# version 09
# baseline (speedup 1.0000x reference)
import numpy as np
from contextlib import ExitStack

import concourse.bass as bass
import concourse.mybir as mybir
from concourse.bass_utils import run_bass_kernel_spmd

F32 = mybir.dt.float32
BF16 = mybir.dt.bfloat16
AF = mybir.ActivationFunctionType
ALU = mybir.AluOpType
AXX = mybir.AxisListType.X

ENGINES = ("PE", "DVE", "ACT", "POOL", "SP")



class Region:
    __slots__ = ("name", "recs")

    def __init__(self, name):
        self.name = name
        self.recs = []


class _Op:
    __slots__ = ("engine", "emit", "deps", "is_dma", "seq", "tag",
                 "sem", "val", "need_inc", "waits")

    def __init__(self, engine, emit, is_dma, tag):
        self.engine = engine
        self.emit = emit
        self.deps = []
        self.is_dma = is_dma
        self.seq = -1
        self.tag = tag
        self.sem = None
        self.val = 0
        self.need_inc = False
        self.waits = []


class Sched:
    def __init__(self, nc, strict=True):
        self.nc = nc
        self.ops = []
        self.regions = {}
        self.strict = strict

    def region(self, name):
        r = self.regions.get(name)
        if r is None:
            r = self.regions[name] = Region(name)
        return r

    def add(self, engine, emit, reads=(), writes=(), dma=False, tag=""):
        op = _Op(engine, emit, dma, tag)
        op.seq = len(self.ops)
        dep_set = {}
        for (reg, a, b) in reads:
            for rec in reg.recs:
                if rec[0] < b and a < rec[1]:
                    w = rec[2]
                    if w is not None:
                        dep_set[w.seq] = w
                    rec[3].append(op)
        for (reg, a, b) in writes:
            keep = []
            for rec in reg.recs:
                if rec[0] < b and a < rec[1]:
                    if rec[2] is not None:
                        dep_set[rec[2].seq] = rec[2]
                    for r_ in rec[3]:
                        dep_set[r_.seq] = r_
                    if rec[0] < a:
                        keep.append([rec[0], a, rec[2], list(rec[3])])
                    if b < rec[1]:
                        keep.append([b, rec[1], rec[2], list(rec[3])])
                else:
                    keep.append(rec)
            keep.append([a, b, op, []])
            reg.recs = keep
        op.deps = [d for d in dep_set.values() if d.seq != op.seq]
        self.ops.append(op)
        return op

    def finalize(self, block, stack):
        nc = self.nc
        for op in self.ops:
            if op.is_dma:
                op.need_inc = True
        for op in self.ops:
            for d in op.deps:
                if d.engine == "PE" and op.engine == "PE" and not d.is_dma \
                        and not op.is_dma:
                    continue
                if self.strict or d.is_dma or d.engine != op.engine:
                    d.need_inc = True
        SEM_MAX = 55000
        NSLOT = 8
        counters = {}
        sems = {}
        dma_idx = {}
        throttle = {}
        for op in self.ops:
            if op.is_dma:
                i = dma_idx.get(op.engine, 0)
                dma_idx[op.engine] = i + 1
                slot = i % NSLOT
                val = 16 * (i // NSLOT + 1)
                skey = (op.engine, "q", slot)
                if skey not in sems:
                    sems[skey] = stack.enter_context(
                        nc.semaphore(f"s_{op.engine}_q{slot}"))
                op.sem = sems[skey]
                op.val = val
                if val > 16:
                    throttle[op] = (op.sem, val - 16)
                continue
            key = (op.engine, "c")
            cnt, epoch = counters.get(key, (0, 0))
            if op.need_inc and cnt + 1 > SEM_MAX:
                epoch += 1
                cnt = 0
            if op.need_inc:
                cnt += 1
            counters[key] = (cnt, epoch)
            skey = key + (epoch,)
            if op.need_inc and skey not in sems:
                sems[skey] = stack.enter_context(
                    nc.semaphore(f"s_{key[0]}_c{epoch}"))
            op.sem = sems.get(skey)
            op.val = cnt
        water = {e: {} for e in ENGINES}
        for op in self.ops:
            w = water[op.engine]
            th = throttle.get(op)
            if th is not None:
                sem, val = th
                if w.get(id(sem), -1) < val:
                    op.waits.append((sem, val))
                    w[id(sem)] = val
            for d in op.deps:
                if (d.engine == "PE" and op.engine == "PE" and not d.is_dma
                        and not op.is_dma):
                    continue
                if not d.need_inc:
                    assert d.engine == op.engine and not d.is_dma
                    continue
                if (not self.strict and d.engine == op.engine
                        and not d.is_dma and not op.is_dma):
                    continue
                sid = id(d.sem)
                if w.get(sid, -1) < d.val:
                    op.waits.append((d.sem, d.val))
                    w[sid] = d.val
        per_engine = {e: [] for e in ENGINES}
        for op in self.ops:
            per_engine[op.engine].append(op)
        for e in ENGINES:
            q = [op for op in per_engine[e] if op.is_dma]
            if q:
                finals = {}
                for d in q:
                    cur = finals.get(id(d.sem))
                    if cur is None or d.val > cur[1]:
                        finals[id(d.sem)] = (d.sem, d.val)

                def mk_drain(pairs):
                    def drain(eng):
                        for sem, val in pairs:
                            eng.wait_ge(sem, val)
                        return None
                    return drain
                op = _Op(e, mk_drain(list(finals.values())), False, "drain")
                op.seq = len(self.ops)
                self.ops.append(op)
                per_engine[e].append(op)

        def mk_body(ename):
            ops = per_engine[ename]

            def body(eng):
                for op in ops:
                    for (sem, val) in op.waits:
                        eng.wait_ge(sem, val)
                    inst = op.emit(eng)
                    if op.need_inc:
                        if inst is None:
                            raise RuntimeError(f"op {op.tag} emitted None")
                        inst.then_inc(op.sem, 16 if op.is_dma else 1)
            return body

        if per_engine["PE"]:
            block.tensor(mk_body("PE"))
        if per_engine["DVE"]:
            block.vector(mk_body("DVE"))
        if per_engine["ACT"]:
            block.scalar(mk_body("ACT"))
        if per_engine["POOL"]:
            block.gpsimd(mk_body("POOL"))
        if per_engine["SP"]:
            block.sync(mk_body("SP"))



REAL_DIMS = dict(B_CORE=2, S=2048, H=2048, KEY=2048, KI=8192, I=1024, K=8,
                 ISH=2048, E=128)


def _chunks(total, step):
    out = []
    o = 0
    while o < total:
        w = min(step, total - o)
        out.append((o, w))
        o += w
    return out


def build_program(nc, D):
    B_CORE, S, H = D["B_CORE"], D["S"], D["H"]
    KEY, KI, I, K, ISH, E = (D["KEY"], D["KI"], D["I"], D["K"], D["ISH"],
                             D["E"])
    NT = S // 128
    HLF = NT // 2
    NHB = H // 128
    NKB = KEY // 128
    NIB = I // 128
    NI2B = ISH // 128
    SH = S // 2
    GW = HLF * 128
    nbn = max(1, H // 512)
    rscale = float(1.0 / np.sqrt(np.float32(KEY)))

    X = nc.dram_tensor("L0_attn_x", [B_CORE, S, H], F32,
                       kind="ExternalInput").ap()
    Wqkv = nc.dram_tensor("L0_attn_qkv", [H, 3 * KEY], F32,
                          kind="ExternalInput").ap()
    Wo = nc.dram_tensor("L0_attn_o_proj", [KEY, H], F32,
                        kind="ExternalInput").ap()
    Gam = nc.dram_tensor("gamma_pb", [128, H // 128], F32,
                         kind="ExternalInput").ap()
    Bet = nc.dram_tensor("beta_pb", [128, H // 128], F32,
                         kind="ExternalInput").ap()
    Wr = nc.dram_tensor("L0_ffn_w_router", [H, E], F32,
                        kind="ExternalInput").ap()
    Wg = nc.dram_tensor("L0_ffn_w_gate", [H, KI], F32,
                        kind="ExternalInput").ap()
    Wu = nc.dram_tensor("L0_ffn_w_up", [H, KI], F32,
                        kind="ExternalInput").ap()
    Wd = nc.dram_tensor("L0_ffn_w_down", [I, H], F32,
                        kind="ExternalInput").ap()
    Amp = nc.dram_tensor("L0_ffn_amplitudes", [E, K, 2], F32,
                         kind="ExternalInput").ap()
    Esc = nc.dram_tensor("L0_ffn_expert_scale", [E], F32,
                         kind="ExternalInput").ap()
    Wsg = nc.dram_tensor("L0_ffn_shared_w_gate", [H, ISH], F32,
                         kind="ExternalInput").ap()
    Wsu = nc.dram_tensor("L0_ffn_shared_w_up", [H, ISH], F32,
                         kind="ExternalInput").ap()
    Wsd = nc.dram_tensor("L0_ffn_shared_w_down", [ISH, H], F32,
                         kind="ExternalInput").ap()
    OUT = nc.dram_tensor("out", [B_CORE, S, H], F32,
                         kind="ExternalOutput").ap()
    q_sp = nc.dram_tensor("q_sp", [B_CORE, NKB, 128, S], BF16,
                          kind="Internal").ap()
    kT_sp = nc.dram_tensor("kT_sp", [B_CORE, NKB, 128, S], BF16,
                           kind="Internal").ap()
    xnT_sp = nc.dram_tensor("xnT_sp", [B_CORE, NHB, 128, S], BF16,
                            kind="Internal").ap()
    u_sp = nc.dram_tensor("u_sp", [B_CORE, S, H], BF16,
                          kind="Internal").ap()
    avT_sp = nc.dram_tensor("avT_sp", [B_CORE, NKB, 128, S], BF16,
                            kind="Internal").ap()
    s_sp = nc.dram_tensor("s_sp", [B_CORE, S, H], BF16,
                          kind="Internal").ap()

    st = ExitStack()
    sbt = lambda n, shp, dt: st.enter_context(nc.sbuf_tensor(n, shp, dt))
    pst = lambda n, shp, dt: st.enter_context(nc.psum_tensor(n, shp, dt))

    VEL = max(NT * KEY,
              2 * HLF * I + I + 2 * 1024 + NIB * GW,
              2 * NIB * 512 + 2 * NI2B * 512 + NIB * GW)
    WEL = max(3 * NHB * 128 + 2 * NHB * 512 + 2 * 512,
              2 * NKB * 512 + 2 * NKB * 128,
              NKB * H,
              4 * NHB * 512,
              NI2B * GW + 4 * NHB * 128)
    FEL = max(NHB * SH,
              2 * S + 2 * S + S + 3 * NKB * 128,
              2 * NKB * 128)
    r_v = sbt("r_v", [128, VEL], BF16)
    r_w = sbt("r_w", [128, WEL], BF16)
    r_feat = sbt("r_feat", [128, FEL], BF16)
    XUW = max(H, 2048)
    r_x = sbt("r_x", [128, 2, XUW], F32)
    r_u = sbt("r_u", [128, 2, XUW], BF16)
    r_stg = sbt("r_stg", [128, 2, 512 + NHB * 128], BF16)

    idbf = sbt("idbf", [128, 128], BF16)
    idf32 = sbt("idf32", [128, 128], F32)
    causal = sbt("causal", [128, 128], F32)
    onesf = sbt("onesf", [1, 128], F32)
    stc = sbt("stc", [128, 2, 16], F32)
    zbuf = sbt("zbuf", [128, max(H, 512)], BF16)
    bnst = sbt("bnst", [128, 2, nbn, 6], F32)
    ag12 = sbt("ag12", [128, 2, 4], F32)
    cc_ = sbt("cc_", [128, 2], F32)
    gam_sb = sbt("gam_sb", [128, NHB], F32)
    bet_sb = sbt("bet_sb", [128, NHB], F32)
    wr_sb = sbt("wr_sb", [128, NHB, E], BF16)
    amp_sb = sbt("amp_sb", [128, 2 * K], F32)
    mixp = sbt("mixp", [128, K], F32)
    mix_bf = sbt("mix_bf", [128, K], BF16)
    es_row = sbt("es_row", [1, E], F32)
    es_b = sbt("es_b", [128, E], F32)
    rtr = sbt("rtr", [128, E], F32)
    cwT = sbt("cwT", [128, 2, E], BF16)
    wk_g = sbt("wk_g", [128, HLF, K], F32)
    nmx = sbt("nmx", [128, 1], F32)
    ssm = sbt("ssm", [128, 1], F32)
    rsb = sbt("rsb", [128, 1], F32)

    mm = [pst(f"mm{i}", [128, 512], F32) for i in range(4)]
    tpp = [pst(f"tp{i}", [128, 1024], BF16) for i in range(4)]

    s = Sched(nc)

    def reg(n):
        return s.region(n)

    RV, RW, RF = reg("v"), reg("w"), reg("feat")
    RX, RU, RS = reg("x"), reg("u"), reg("stg")
    SMALLS = ("idbf idf32 causal onesf statT bcb stc bnst ag12 cc gam bet "
              "wr amp mixp mixt mixbf esrow esb rtr prb prw eqm mskp cwbf "
              "cwT wkg zbuf nmx ssm rsb").split()
    RSM = {n: reg(n) for n in SMALLS}
    PS = {n: reg(n) for n in
          ("mm0", "mm1", "mm2", "mm3", "tp0", "tp1", "tp2", "tp3")}
    RD = {n: reg(n) for n in
          ("d_q", "d_k", "d_xnT", "d_u", "d_avT", "d_s", "d_out")}
    MMR = [(PS[f"mm{i}"], 0, 1) for i in range(4)]
    TPR = [(PS[f"tp{i}"], 0, 1) for i in range(4)]

    def sm(n, a=0, b=1 << 30):
        return (RSM[n], a, b)

    def dr(n, b_):
        return (RD[n], b_, b_ + 1)

    A = s.add

    vres = r_v[:, 0:NT * KEY].rearrange("p (t d) -> p t d", d=KEY)
    featA = r_feat[:, 0:NHB * SH].rearrange("p (h t) -> p h t", t=SH)
    wqkv_st = [r_w[:, j * NHB * 128:(j + 1) * NHB * 128]
               .rearrange("p (h c) -> p h c", c=128) for j in range(3)]
    VSTO = 3 * NHB * 128
    v_st = [r_w[:, VSTO + j * NHB * 512:VSTO + (j + 1) * NHB * 512]
            .rearrange("p (h c) -> p h c", c=512) for j in range(2)]
    KSTO = VSTO + 2 * NHB * 512
    kstage = [r_w[:, KSTO + j * 512:KSTO + (j + 1) * 512] for j in range(2)]
    k_b = [r_w[:, j * NKB * 512:(j + 1) * NKB * 512]
           .rearrange("p (d c) -> p d c", c=512) for j in range(2)]
    AVSO = 2 * NKB * 512
    wo_full = r_w[:, 0:NKB * H].rearrange("p (d c) -> p d c", c=H)
    scores = r_feat[:, 0:2 * S].bitcast(F32)
    pbuf = r_feat[:, 2 * S:4 * S].bitcast(F32)
    probs_bf = r_feat[:, 4 * S:5 * S]
    probsT = r_feat[:, 5 * S:5 * S + NKB * 128].rearrange(
        "p (d t) -> p d t", t=128)
    QTO = 5 * S + NKB * 128
    qtile = [r_feat[:, QTO + j * NKB * 128:QTO + (j + 1) * NKB * 128]
             .rearrange("p (d t) -> p d t", t=128) for j in range(2)]
    stg_flat = r_stg[:, :, :].rearrange("p a b -> p (a b)")
    avstage = [stg_flat[:, j * NKB * 128:(j + 1) * NKB * 128]
               .rearrange("p (d c) -> p d c", c=128) for j in range(2)]
    av_b = [stg_flat[:, j * NKB * 128:(j + 1) * NKB * 128]
            .rearrange("p (d t) -> p d t", t=128) for j in range(2)]
    rtr8 = stg_flat[:, 0:2 * HLF * E].bitcast(F32).rearrange(
        "p (t e) -> p t e", e=E)
    prw8 = stg_flat[:, 2 * HLF * E:4 * HLF * E].bitcast(F32).rearrange(
        "p (t e) -> p t e", e=E)
    cw8 = stg_flat[:, 4 * HLF * E:5 * HLF * E].rearrange(
        "p (t e) -> p t e", e=E)
    wg_st = [r_w[:, j * NHB * 512:(j + 1) * NHB * 512]
             .rearrange("p (h c) -> p h c", c=512) for j in range(2)]
    wu_st = [r_w[:, (2 + j) * NHB * 512:(3 + j) * NHB * 512]
             .rearrange("p (h c) -> p h c", c=512) for j in range(2)]
    hshT = r_w[:, 0:NI2B * GW].rearrange("p (d t) -> p d t", t=GW)
    WSGO = NI2B * GW
    wsg_st = [r_w[:, WSGO + j * NHB * 128:WSGO + (j + 1) * NHB * 128]
              .rearrange("p (h c) -> p h c", c=128) for j in range(2)]
    WSUO = WSGO + 2 * NHB * 128
    wsu_st = [r_w[:, WSUO + j * NHB * 128:WSUO + (j + 1) * NHB * 128]
              .rearrange("p (h c) -> p h c", c=128) for j in range(2)]
    cacc = r_v[:, 0:2 * HLF * I].bitcast(F32).rearrange(
        "p (t i) -> p t i", i=I)
    CBO = 2 * HLF * I
    cbf = r_v[:, CBO:CBO + I]
    SG2O = CBO + I
    sgs2 = [r_v[:, SG2O + j * 1024:SG2O + (j + 1) * 1024].bitcast(F32)
            for j in range(2)]
    CTO = VEL - NIB * GW
    cT = r_v[:, CTO:CTO + NIB * GW].rearrange("p (d t) -> p d t", t=GW)
    wd_st = [r_v[:, j * NIB * 512:(j + 1) * NIB * 512]
             .rearrange("p (d c) -> p d c", c=512) for j in range(2)]
    WSDO = 2 * NIB * 512
    wsd_st = [r_v[:, WSDO + j * NI2B * 512:WSDO + (j + 1) * NI2B * 512]
              .rearrange("p (d c) -> p d c", c=512) for j in range(2)]
    assert WSDO + 2 * NI2B * 512 <= CTO, "wd/wsd overlap cT"
    u_sl = [r_u[:, j, 0:512] for j in range(2)]
    s_sl = [r_u[:, j, 512:1024] for j in range(2)]
    sstage = [r_u[:, j, 1024:1536] for j in range(2)]
    sgsD = [r_u[:, j, 0:1024].bitcast(F32) for j in range(2)]
    tbD = [r_u[:, j, 1024:2048].bitcast(F32) for j in range(2)]
    x_sl = [r_x[:, j, 0:512] for j in range(2)]
    gsb = [r_x[:, j, 512:1024] for j in range(2)]
    tb2 = [r_x[:, j, 1024:1536] for j in range(2)]
    ostage = [r_x[:, j, 1536:2048] for j in range(2)]
    qstage = [r_stg[:, j, 0:512] for j in range(2)]
    xnstage = [r_stg[:, j, 512:512 + NHB * 128].rearrange(
        "p (h c) -> p h c", c=128) for j in range(2)]

    def ru_iv(par, a, b):
        return (RU, par * XUW + a, par * XUW + b)

    def rx_iv(par, a, b):
        return (RX, par * XUW + a, par * XUW + b)

    def rstg_iv(par, a, b):
        W2 = 512 + NHB * 128
        return (RS, par * W2 + a, par * W2 + b)

    A("POOL", lambda e: e.memset(cc_[:, 0:1], 1e-5), writes=[sm("cc", 0, 1)])
    A("POOL", lambda e: e.memset(cc_[:, 1:2], -0.5), writes=[sm("cc", 1, 2)])
    A("POOL", lambda e: e.memset(idbf[:, :], 0.0), writes=[sm("idbf")])
    A("POOL", lambda e: e.affine_select(
        out=idbf[:, :], in_=idbf[:, :], compare_op=ALU.not_equal, fill=1.0,
        base=0, pattern=[[-1, 128]], channel_multiplier=1),
        reads=[sm("idbf")], writes=[sm("idbf")])
    A("POOL", lambda e: e.memset(idf32[:, :], 0.0), writes=[sm("idf32")])
    A("POOL", lambda e: e.affine_select(
        out=idf32[:, :], in_=idf32[:, :], compare_op=ALU.not_equal, fill=1.0,
        base=0, pattern=[[-1, 128]], channel_multiplier=1),
        reads=[sm("idf32")], writes=[sm("idf32")])
    A("POOL", lambda e: e.memset(causal[:, :], 0.0), writes=[sm("causal")])
    A("POOL", lambda e: e.affine_select(
        out=causal[:, :], in_=causal[:, :], compare_op=ALU.is_ge, fill=-1e30,
        base=0, pattern=[[-1, 128]], channel_multiplier=1),
        reads=[sm("causal")], writes=[sm("causal")])
    A("POOL", lambda e: e.memset(onesf[:, :], 1.0), writes=[sm("onesf")])
    A("POOL", lambda e: e.dma_start(
        out=wr_sb[:, :, :], in_=Wr.rearrange("(b p) c -> p b c", p=128)),
        writes=[sm("wr")], dma=True)
    A("SP", lambda e: e.dma_start(out=gam_sb[:, :], in_=Gam[:, :]),
      writes=[sm("gam")], dma=True)
    A("SP", lambda e: e.dma_start(out=bet_sb[:, :], in_=Bet[:, :]),
      writes=[sm("bet")], dma=True)
    A("SP", lambda e: e.dma_start(
        out=amp_sb[:, :], in_=Amp.rearrange("e k t -> e (k t)")),
        writes=[sm("amp")], dma=True)
    A("SP", lambda e: e.dma_start(out=es_row[0:1, :], in_=Esc[None, :]),
      writes=[sm("esrow")], dma=True)
    A("DVE", lambda e: e.tensor_mul(mixp[:, :], amp_sb[:, 0:2 * K:2],
                                    amp_sb[:, 0:2 * K:2]),
      reads=[sm("amp")], writes=[sm("mixp")])
    A("DVE", lambda e: e.tensor_mul(rtr[:, 0:K], amp_sb[:, 1:2 * K:2],
                                    amp_sb[:, 1:2 * K:2]),
      reads=[sm("amp")], writes=[sm("rtr")])
    A("DVE", lambda e: e.tensor_add(mixp[:, :], mixp[:, :], rtr[:, 0:K]),
      reads=[sm("mixp"), sm("rtr")], writes=[sm("mixp")])
    A("DVE", lambda e: e.reduce_sum(stc[:, 0, 0:1], mixp[:, :], axis=AXX),
      reads=[sm("mixp")], writes=[sm("stc", 0, 1)])
    A("DVE", lambda e: e.tensor_scalar_add(stc[:, 0, 0:1], stc[:, 0, 0:1], 1e-8),
      reads=[sm("stc", 0, 1)], writes=[sm("stc", 0, 1)])
    A("DVE", lambda e: e.reciprocal(stc[:, 0, 1:2], stc[:, 0, 0:1]),
      reads=[sm("stc", 0, 1)], writes=[sm("stc", 1, 2)])
    A("DVE", lambda e: e.tensor_scalar(mix_bf[:, :], mixp[:, :], stc[:, 0, 1:2],
                                       None, ALU.mult),
      reads=[sm("mixp"), sm("stc", 1, 2)], writes=[sm("mixbf")])
    A("PE", lambda e: e.matmul(mm[0][0:128, 0:E], onesf[0:1, 0:128],
                               es_row[0:1, :], start=True, stop=True),
      reads=[sm("onesf"), sm("esrow")], writes=[MMR[0]])
    A("DVE", lambda e: e.tensor_copy(es_b[:, :], mm[0][0:128, 0:E]),
      reads=[MMR[0]], writes=[sm("esb")])

    def emit_batch(b):
        zb_v = [r_x[:, j, 0:(H + 1) // 2].bitcast(BF16) for j in range(2)]

        def a1_head(t):
            par = t % 2
            A("SP", lambda e, tt=t, pp=par: e.dma_start(
                out=r_x[:, pp, 0:H],
                in_=X[b, tt * 128:(tt + 1) * 128, :]),
                writes=[rx_iv(par, 0, H)], dma=True, tag=f"ldx{t}")
            for c, (off, wd_) in enumerate(_chunks(H, 512)):
                A("DVE", lambda e, pp=par, cc=c, oo=off, ww=wd_:
                  e.bn_stats(bnst[:, pp, cc, :], r_x[:, pp, oo:oo + ww]),
                  reads=[rx_iv(par, 0, H)],
                  writes=[sm("bnst", par * nbn + c, par * nbn + c + 1)])
            A("DVE", lambda e, pp=par: e.bn_aggr(ag12[:, pp, 0:2],
                                                 bnst[:, pp, :, :]),
              reads=[sm("bnst", par * nbn, par * nbn + nbn)],
              writes=[sm("ag12", par * 4, par * 4 + 2)])
            A("ACT", lambda e, pp=par: e.activation(
                stc[:, pp, 2:3], ag12[:, pp, 1:2], AF.Ln,
                bias=cc_[:, 0:1]),
                reads=[sm("ag12", par * 4, par * 4 + 2), sm("cc", 0, 1)],
                writes=[sm("stc", par * 16 + 2, par * 16 + 3)])
            A("ACT", lambda e, pp=par: e.activation(
                stc[:, pp, 3:4], stc[:, pp, 2:3], AF.Exp,
                scale=cc_[:, 1:2]),
                reads=[sm("stc", par * 16 + 2, par * 16 + 3),
                       sm("cc", 1, 2)],
                writes=[sm("stc", par * 16 + 3, par * 16 + 4)])
            A("DVE", lambda e, pp=par: e.tensor_scalar(
                r_u[:, pp, 0:H], r_x[:, pp, 0:H], ag12[:, pp, 0:1],
                stc[:, pp, 3:4], ALU.subtract, ALU.mult),
                reads=[rx_iv(par, 0, H), sm("ag12", par * 4, par * 4 + 2),
                       sm("stc", par * 16 + 3, par * 16 + 4)],
                writes=[ru_iv(par, 0, H)], tag=f"xn{t}")
            A("SP", lambda e, tt=t, pp=par: e.dma_start(
                out=u_sp[b, tt * 128:(tt + 1) * 128, :],
                in_=r_u[:, pp, 0:H]),
                reads=[ru_iv(par, 0, H)],
                writes=[(RD["d_u"], b * NT + t, b * NT + t + 1)],
                dma=True, tag=f"spu{t}")
            for c, (off, wd_) in enumerate(_chunks(H, 512)):
                A("DVE", lambda e, pp=par, cc=c, oo=off, ww=wd_:
                  e.bn_stats(bnst[:, pp, cc, :], r_u[:, pp, oo:oo + ww]),
                  reads=[ru_iv(par, 0, H)],
                  writes=[sm("bnst", par * nbn + c, par * nbn + c + 1)])
            A("DVE", lambda e, pp=par: e.bn_aggr(ag12[:, pp, 2:4],
                                                 bnst[:, pp, :, :]),
              reads=[sm("bnst", par * nbn, par * nbn + nbn)],
              writes=[sm("ag12", par * 4 + 2, par * 4 + 4)])
            A("ACT", lambda e, pp=par: e.activation(
                stc[:, pp, 6:7], ag12[:, pp, 3:4], AF.Ln,
                bias=cc_[:, 0:1]),
                reads=[sm("ag12", par * 4 + 2, par * 4 + 4),
                       sm("cc", 0, 1)],
                writes=[sm("stc", par * 16 + 6, par * 16 + 7)])
            A("ACT", lambda e, pp=par: e.activation(
                stc[:, pp, 7:8], stc[:, pp, 6:7], AF.Exp,
                scale=cc_[:, 1:2]),
                reads=[sm("stc", par * 16 + 6, par * 16 + 7),
                       sm("cc", 1, 2)],
                writes=[sm("stc", par * 16 + 7, par * 16 + 8)])
            A("DVE", lambda e, pp=par: e.tensor_scalar(
                zb_v[pp][:, 0:H], r_u[:, pp, 0:H], ag12[:, pp, 2:3],
                stc[:, pp, 7:8], ALU.subtract, ALU.mult),
                reads=[ru_iv(par, 0, H),
                       sm("ag12", par * 4 + 2, par * 4 + 4),
                       sm("stc", par * 16 + 7, par * 16 + 8)],
                writes=[rx_iv(par, 0, (H + 1) // 2)], tag=f"z{t}")

        def a1_tail(t):
            par = t % 2
            ti = t % HLF
            for hb in range(NHB):
                txn = hb % 2
                tz = 2 + hb % 2
                A("PE", lambda e, pp=par, hh=hb, tj=txn: e.transpose(
                    tpp[tj][:, 0:128],
                    r_u[:, pp, hh * 128:(hh + 1) * 128], idbf[:, :]),
                    reads=[ru_iv(par, 0, H), sm("idbf")],
                    writes=[TPR[txn]], tag=f"utp{t}_{hb}")
                A("ACT", lambda e, pp=par, hh=hb, tj=txn:
                  e.copy(xnstage[pp][:, hh, :], tpp[tj][:, 0:128]),
                  reads=[TPR[txn]],
                  writes=[rstg_iv(par, 512 + hb * 128,
                                  512 + hb * 128 + 128)])
                A("PE", lambda e, pp=par, hh=hb, tj=tz: e.transpose(
                    tpp[tj][:, 0:128],
                    zb_v[pp][:, hh * 128:(hh + 1) * 128], idbf[:, :]),
                    reads=[rx_iv(par, 0, (H + 1) // 2), sm("idbf")],
                    writes=[TPR[tz]], tag=f"ztp{t}_{hb}")
                A("DVE", lambda e, hh=hb, ti_=ti, tj=tz: e.tensor_scalar(
                    featA[:, hh, ti_ * 128:(ti_ + 1) * 128],
                    tpp[tj][:, 0:128], gam_sb[:, hh:hh + 1],
                    bet_sb[:, hh:hh + 1], ALU.mult, ALU.add),
                    reads=[TPR[tz], sm("gam"), sm("bet")],
                    writes=[(RF, hb * SH + ti * 128,
                             hb * SH + ti * 128 + 128)],
                    tag=f"xnorm{t}_{hb}")
            A("SP", lambda e, tt=t, pp=par: e.dma_start(
                out=xnT_sp[b, :, :, tt * 128:(tt + 1) * 128].rearrange(
                    "h p t -> p h t"),
                in_=xnstage[pp][:, :, :]),
                reads=[rstg_iv(par, 512, 512 + NHB * 128)],
                writes=[(RD["d_xnT"], b * 2 + t // HLF,
                         b * 2 + t // HLF + 1)],
                dma=True, tag=f"spxnT{t}")

        for half in range(2):
            t0 = half * HLF
            a1_head(t0)
            for ti in range(HLF):
                if ti + 1 < HLF:
                    a1_head(t0 + ti + 1)
                a1_tail(t0 + ti)
            FEAT_ALL = (RF, 0, NHB * SH)
            rot = 0
            for cc in range(2 * NKB):
                wj = cc % 3
                c0 = cc * 128
                WIV = (RW, wj * NHB * 128, (wj + 1) * NHB * 128)
                A("POOL", lambda e, jj=wj, c_=c0: e.dma_start(
                    out=wqkv_st[jj][:, :, :],
                    in_=Wqkv[:, c_:c_ + 128].rearrange(
                        "(b p) c -> p b c", p=128)),
                    writes=[WIV], dma=True, tag=f"ldqkv{cc}")
                for ts_, (toff, tw) in enumerate(_chunks(SH, 512)):
                    bank = rot % 4
                    rot += 1

                    def qk_group(e, jj=wj, bk=bank, to=toff, tw_=tw):
                        last = None
                        for hh in range(NHB):
                            last = e.matmul(
                                mm[bk][:, 0:tw_], wqkv_st[jj][:, hh, :],
                                featA[:, hh, to:to + tw_],
                                start=(hh == 0), stop=(hh == NHB - 1))
                        return last
                    A("PE", qk_group, reads=[WIV, FEAT_ALL],
                      writes=[MMR[bank]], tag=f"qkmm{cc}_{ts_}")
                    gtoff = half * SH + toff
                    qp = rot % 2
                    if cc < NKB:
                        A("DVE", lambda e, bk=bank, qq=qp, tw_=tw:
                          e.tensor_copy(qstage[qq][:, 0:tw_],
                                        mm[bk][:, 0:tw_]),
                          reads=[MMR[bank]], writes=[rstg_iv(qp, 0, 512)])
                        A("SP", lambda e, c_=cc, qq=qp, go=gtoff, tw_=tw:
                          e.dma_start(out=q_sp[b, c_, :, go:go + tw_],
                                      in_=qstage[qq][:, 0:tw_]),
                          reads=[rstg_iv(qp, 0, 512)],
                          writes=[(RD["d_q"], b * 2 + half,
                                   b * 2 + half + 1)], dma=True)
                    else:
                        dk = cc - NKB
                        A("DVE", lambda e, bk=bank, qq=qp, tw_=tw:
                          e.tensor_copy(kstage[qq][:, 0:tw_],
                                        mm[bk][:, 0:tw_]),
                          reads=[MMR[bank]],
                          writes=[(RW, KSTO + qp * 512,
                                   KSTO + qp * 512 + 512)])
                        A("SP", lambda e, d_=dk, qq=qp, go=gtoff, tw_=tw:
                          e.dma_start(out=kT_sp[b, d_, :, go:go + tw_],
                                      in_=kstage[qq][:, 0:tw_]),
                          reads=[(RW, KSTO + qp * 512, KSTO + qp * 512 + 512)],
                          writes=[(RD["d_k"], b * 2 + half,
                                   b * 2 + half + 1)], dma=True)
            for vc, (vo, vw) in enumerate(_chunks(KEY, 512)):
                vj = vc % 2
                VIV = (RW, VSTO + vj * NHB * 512, VSTO + (vj + 1) * NHB * 512)
                A("POOL", lambda e, jj=vj, vo_=vo, vw_=vw: e.dma_start(
                    out=v_st[jj][:, :, 0:vw_],
                    in_=Wqkv[:, 2 * KEY + vo_:2 * KEY + vo_ + vw_].rearrange(
                        "(b p) c -> p b c", p=128)),
                    writes=[VIV], dma=True, tag=f"ldv{vc}")
                for ti in range(HLF):
                    t = t0 + ti
                    bank = rot % 4
                    rot += 1

                    def v_group(e, jj=vj, bk=bank, ti_=ti, vw_=vw):
                        last = None
                        for hh in range(NHB):
                            last = e.matmul(
                                mm[bk][:, 0:vw_],
                                featA[:, hh, ti_ * 128:(ti_ + 1) * 128],
                                v_st[jj][:, hh, 0:vw_],
                                start=(hh == 0), stop=(hh == NHB - 1))
                        return last
                    A("PE", v_group, reads=[VIV, FEAT_ALL],
                      writes=[MMR[bank]], tag=f"vmm{vc}_{t}")
                    A("DVE", lambda e, bk=bank, tt=t, vo_=vo, vw_=vw:
                      e.tensor_copy(vres[:, tt, vo_:vo_ + vw_],
                                    mm[bk][:, 0:vw_]),
                      reads=[MMR[bank]],
                      writes=[(RV, t * KEY + vo, t * KEY + vo + vw)])

        V_ALL = (RV, 0, NT * KEY)
        scoresP = [scores, pbuf]

        def b_scores_pair(p2):
            iA, iB = 2 * p2, 2 * p2 + 1
            LB = (iB + 1) * 128
            for i_, qp in ((iA, 0), (iB, 1)):
                QIV = (RF, QTO + qp * NKB * 128, QTO + (qp + 1) * NKB * 128)
                A("SP", lambda e, ii=i_, qq=qp: e.dma_start(
                    out=qtile[qq][:, :, :],
                    in_=q_sp[b, :, :, ii * 128:(ii + 1) * 128].rearrange(
                        "d p t -> p d t")),
                    reads=[(RD["d_q"], b * 2 + i_ // HLF,
                            b * 2 + i_ // HLF + 1)],
                    writes=[QIV], dma=True, tag=f"ldq{i_}")
            for c, (soff, swd) in enumerate(_chunks(LB, 512)):
                kj = c % 2
                KIV = (RW, kj * NKB * 512, (kj + 1) * NKB * 512)
                A("POOL" if c % 2 == 0 else "SP",
                  lambda e, jj=kj, so=soff, sw=swd: e.dma_start(
                      out=k_b[jj][:, :, 0:sw],
                      in_=kT_sp[b, :, :, so:so + sw].rearrange(
                          "d p t -> p d t")),
                  reads=[(RD["d_k"], b * 2 + soff // SH,
                          b * 2 + (soff + swd - 1) // SH + 1)],
                  writes=[KIV], dma=True, tag=f"ldk{p2}_{c}")
                for i_, qp in ((iA, 0), (iB, 1)):
                    L = (i_ + 1) * 128
                    if soff >= L:
                        continue
                    sw = min(swd, L - soff)
                    bank = qp
                    QIV = (RF, QTO + qp * NKB * 128,
                           QTO + (qp + 1) * NKB * 128)

                    def sc_group(e, jj=kj, bk=bank, sw_=sw, qq=qp):
                        last = None
                        for dk_ in range(NKB):
                            last = e.matmul(
                                mm[bk][:, 0:sw_], qtile[qq][:, dk_, :],
                                k_b[jj][:, dk_, 0:sw_],
                                start=(dk_ == 0), stop=(dk_ == NKB - 1))
                        return last
                    A("PE", sc_group, reads=[QIV, KIV], writes=[MMR[bank]],
                      tag=f"sc{i_}_{c}")
                    sbuf_sc = scoresP[qp]
                    SCV = (RF, qp * 2 * S, (qp + 1) * 2 * S)
                    dstart = L - 128
                    if soff + sw <= dstart:
                        A("DVE", lambda e, bk=bank, so=soff, sw_=sw, sb=sbuf_sc:
                          e.tensor_scalar(sb[:, so:so + sw_],
                                          mm[bk][:, 0:sw_], rscale, None,
                                          ALU.mult),
                          reads=[MMR[bank]], writes=[SCV],
                          tag=f"scev{i_}_{c}")
                    else:
                        w0 = dstart - soff
                        if w0 > 0:
                            A("DVE", lambda e, bk=bank, so=soff, w_=w0,
                              sb=sbuf_sc:
                              e.tensor_scalar(sb[:, so:so + w_],
                                              mm[bk][:, 0:w_], rscale, None,
                                              ALU.mult),
                              reads=[MMR[bank]], writes=[SCV])
                        A("DVE", lambda e, bk=bank, so=soff, w_=w0,
                          sb=sbuf_sc:
                          e.scalar_tensor_tensor(
                              sb[:, so + w_:so + w_ + 128],
                              mm[bk][:, w_:w_ + 128], rscale, causal[:, :],
                              ALU.mult, ALU.add),
                          reads=[MMR[bank], sm("causal")], writes=[SCV],
                          tag=f"scdg{i_}")

        def b_softmax(i):
            L = (i + 1) * 128
            qp = i % 2
            sb = scoresP[qp]
            SCV = (RF, qp * 2 * S, (qp + 1) * 2 * S)
            A("DVE", lambda e, LL=L, sb_=sb: e.reduce_max(
                nmx[:, :], sb_[:, 0:LL], axis=AXX, negate=True),
                reads=[SCV], writes=[sm("nmx")])
            A("ACT", lambda e, LL=L, sb_=sb: e.activation(
                sb_[:, 0:LL], sb_[:, 0:LL], AF.Exp, bias=nmx[:, 0:1],
                accum_out=ssm[:, :]),
                reads=[SCV, sm("nmx")],
                writes=[SCV, sm("ssm")], tag=f"exp{i}")
            A("DVE", lambda e: e.reciprocal(rsb[:, :], ssm[:, :]),
              reads=[sm("ssm")], writes=[sm("rsb")])
            A("DVE", lambda e, LL=L, sb_=sb: e.tensor_scalar(
                probs_bf[:, 0:LL], sb_[:, 0:LL], rsb[:, 0:1], None,
                ALU.mult),
                reads=[SCV, sm("rsb")],
                writes=[(RF, 4 * S, 5 * S)], tag=f"pn{i}")

        def b_probsT(i):
            for tkb in range(i + 1):
                tj = tkb % 4
                A("PE", lambda e, tk=tkb, tj_=tj: e.transpose(
                    tpp[tj_][:, 0:128],
                    probs_bf[:, tk * 128:(tk + 1) * 128], idbf[:, :]),
                    reads=[(RF, 4 * S, 5 * S), sm("idbf")],
                    writes=[TPR[tj]], tag=f"ptp{i}_{tkb}")
                A("ACT", lambda e, tk=tkb, tj_=tj: e.copy(
                    probsT[:, tk, :], tpp[tj_][:, 0:128]),
                    reads=[TPR[tj]],
                    writes=[(RF, 5 * S + tkb * 128, 5 * S + tkb * 128 + 128)])

        def b_av(i):
            PT = (RF, 5 * S, 5 * S + (i + 1) * 128)
            apar = i % 2
            for dc in range(NKB):
                bank = 2 + dc % 2

                def av_group(e, dc_=dc, bk=bank, ii=i):
                    last = None
                    for tk in range(ii + 1):
                        last = e.matmul(
                            mm[bk][:, 0:128],
                            vres[:, tk, dc_ * 128:(dc_ + 1) * 128],
                            probsT[:, tk, :],
                            start=(tk == 0), stop=(tk == ii))
                    return last
                A("PE", av_group, reads=[V_ALL, PT], writes=[MMR[bank]],
                  tag=f"av{i}_{dc}")
                A("DVE", lambda e, dc_=dc, bk=bank, ap=apar: e.tensor_copy(
                    avstage[ap][:, dc_, :], mm[bk][:, 0:128]),
                    reads=[MMR[bank]],
                    writes=[(RS, apar * NKB * 128 + dc * 128,
                             apar * NKB * 128 + dc * 128 + 128)])
            A("SP", lambda e, ii=i, ap=apar: e.dma_start(
                out=avT_sp[b, :, :, ii * 128:(ii + 1) * 128].rearrange(
                    "d p t -> p d t"),
                in_=avstage[ap][:, :, :]),
                reads=[(RS, apar * NKB * 128, (apar + 1) * NKB * 128)],
                writes=[(RD["d_avT"], b * NT + i, b * NT + i + 1)],
                dma=True, tag=f"spav{i}")

        prev = None
        for p2 in range(NT // 2):
            iA, iB = 2 * p2, 2 * p2 + 1
            b_scores_pair(p2)
            b_softmax(iA)
            if prev is not None:
                b_av(prev)
            b_probsT(iA)
            b_softmax(iB)
            b_av(iA)
            b_probsT(iB)
            prev = iB
            if p2 == NT // 2 - 1:
                for hc_, (ho, hw) in enumerate(_chunks(H, 512)):
                    A("POOL", lambda e, ho_=ho, hw_=hw: e.dma_start(
                        out=wo_full[:, :, ho_:ho_ + hw_],
                        in_=Wo[:, ho_:ho_ + hw_].rearrange(
                            "(b p) c -> p b c", p=128)),
                        writes=[(RW, 0, NKB * H)], dma=True,
                        tag=f"ldwo{hc_}")
        b_av(prev)

        WOIV = (RW, 0, NKB * H)
        u_full = [r_u[:, j, 0:H] for j in range(2)]
        s_full = [r_x[:, j, 512:512 + (H + 1) // 2].bitcast(BF16)
                  for j in range(2)]
        for ti in range(NT):
            par = ti % 2
            AIV = (RS, par * NKB * 128, (par + 1) * NKB * 128)
            A("SP", lambda e, tt=ti, pp=par: e.dma_start(
                out=av_b[pp][:, :, :],
                in_=avT_sp[b, :, :, tt * 128:(tt + 1) * 128].rearrange(
                    "d p t -> p d t")),
                reads=[(RD["d_avT"], b * NT + ti, b * NT + ti + 1)],
                writes=[AIV], dma=True, tag=f"ldav{ti}")
            A("ACT", lambda e, tt=ti, pp=par: e.dma_start(
                out=u_full[pp][0:128, :],
                in_=u_sp[b, tt * 128:(tt + 1) * 128, :]),
                reads=[(RD["d_u"], b * NT + ti, b * NT + ti + 1)],
                writes=[ru_iv(par, 0, H)], dma=True)
            for hc, (ho, hw) in enumerate(_chunks(H, 512)):
                bank = (ti * len(_chunks(H, 512)) + hc) % 4

                def o_group(e, tt=ti, pp=par, bk=bank, ho_=ho, hw_=hw):
                    last = None
                    for dk_ in range(NKB):
                        last = e.matmul(
                            mm[bk][:, 0:hw_], av_b[pp][:, dk_, :],
                            wo_full[:, dk_, ho_:ho_ + hw_],
                            start=(dk_ == 0), stop=(dk_ == NKB - 1))
                    return last
                A("PE", o_group, reads=[AIV, WOIV], writes=[MMR[bank]],
                  tag=f"omm{ti}_{hc}")
                A("DVE", lambda e, bk=bank, pp=par, ho_=ho, hw_=hw:
                  e.tensor_tensor(
                      s_full[pp][0:128, ho_:ho_ + hw_], mm[bk][:, 0:hw_],
                      u_full[pp][0:128, ho_:ho_ + hw_], ALU.add),
                  reads=[MMR[bank], ru_iv(par, 0, H)],
                  writes=[rx_iv(par, 512, 512 + (H + 1) // 2)],
                  tag=f"sev{ti}_{hc}")
            A("ACT", lambda e, tt=ti, pp=par: e.dma_start(
                out=s_sp[b, tt * 128:(tt + 1) * 128, :],
                in_=s_full[pp][0:128, :]),
                reads=[rx_iv(par, 512, 512 + (H + 1) // 2)],
                writes=[(RD["d_s"], b * NT + ti, b * NT + ti + 1)],
                dma=True)

        for g in range(2):
            gt0 = g * HLF
            FEAT_G = (RF, 0, NHB * SH)
            A("ACT", lambda e, gg=g: e.dma_start(
                out=featA[:, :, 0:GW],
                in_=xnT_sp[b, :, :, gg * GW:(gg + 1) * GW].rearrange(
                    "h p t -> p h t")),
                reads=[(RD["d_xnT"], b * 2 + g, b * 2 + g + 1)],
                writes=[FEAT_G], dma=True, tag=f"ldxnT{g}")
            R8 = (RS, 0, 2 * HLF * E)
            P8 = (RS, 2 * HLF * E, 4 * HLF * E)
            C8 = (RS, 4 * HLF * E, 5 * HLF * E)
            nm8 = stc[:, 0, 8:8 + HLF]
            sm8 = stc[:, 1, 8:8 + HLF]

            def bc8(ap2d):
                return ap2d[:, :, None].broadcast_to([128, HLF, E])
            for ti in range(HLF):
                bank = ti % 4

                def r_group(e, ti_=ti, bk=bank):
                    last = None
                    for hh in range(NHB):
                        last = e.matmul(
                            mm[bk][:, 0:E],
                            featA[:, hh, ti_ * 128:(ti_ + 1) * 128],
                            wr_sb[:, hh, :], start=(hh == 0),
                            stop=(hh == NHB - 1))
                    return last
                A("PE", r_group, reads=[FEAT_G, sm("wr")],
                  writes=[MMR[bank]], tag=f"rtr{ti}")
                A("ACT", lambda e, ti_=ti, bk=bank: e.copy(
                    rtr8[:, ti_, :], mm[bk][:, 0:E]),
                    reads=[MMR[bank]], writes=[R8], tag=f"rtrev{ti}")
            A("DVE", lambda e: e.reduce_max(nm8, rtr8[:, :, :], axis=AXX),
              reads=[R8], writes=[sm("stc", 8, 16)])
            A("DVE", lambda e: e.tensor_tensor(
                rtr8[:, :, :], rtr8[:, :, :], bc8(nm8), ALU.subtract),
                reads=[R8, sm("stc", 8, 16)], writes=[R8])
            A("ACT", lambda e: e.activation(rtr8[:, :, :], rtr8[:, :, :],
                                            AF.Exp),
              reads=[R8], writes=[R8], tag="rexp")
            A("DVE", lambda e: e.reduce_sum(sm8, rtr8[:, :, :], axis=AXX),
              reads=[R8], writes=[sm("stc", 24, 32)])
            A("DVE", lambda e: e.reciprocal(sm8, sm8),
              reads=[sm("stc", 24, 32)], writes=[sm("stc", 24, 32)])
            A("DVE", lambda e: e.tensor_tensor(
                rtr8[:, :, :], rtr8[:, :, :], bc8(sm8), ALU.mult),
                reads=[R8, sm("stc", 24, 32)], writes=[R8])
            A("DVE", lambda e: e.tensor_copy(prw8[:, :, :], rtr8[:, :, :]),
              reads=[R8], writes=[P8])
            for j in range(4):
                A("DVE", lambda e: e.reduce_max(nm8, prw8[:, :, :],
                                                axis=AXX),
                  reads=[P8], writes=[sm("stc", 8, 16)])
                if j < 3:
                    A("DVE", lambda e: e.tensor_tensor(
                        cw8[:, :, :], prw8[:, :, :], bc8(nm8), ALU.is_lt),
                        reads=[P8, sm("stc", 8, 16)], writes=[C8])
                    A("DVE", lambda e: e.tensor_tensor(
                        prw8[:, :, :], prw8[:, :, :], cw8[:, :, :],
                        ALU.mult),
                        reads=[P8, C8], writes=[P8])
            A("DVE", lambda e: e.tensor_tensor(
                prw8[:, :, :], rtr8[:, :, :], bc8(nm8), ALU.is_ge),
                reads=[R8, sm("stc", 8, 16)], writes=[P8])
            A("DVE", lambda e: e.tensor_tensor(
                prw8[:, :, :], prw8[:, :, :], rtr8[:, :, :], ALU.mult),
                reads=[P8, R8], writes=[P8])
            A("DVE", lambda e: e.reduce_sum(sm8, prw8[:, :, :], axis=AXX),
              reads=[P8], writes=[sm("stc", 24, 32)])
            A("DVE", lambda e: e.tensor_scalar_add(sm8, sm8, 1e-8),
              reads=[sm("stc", 24, 32)], writes=[sm("stc", 24, 32)])
            A("DVE", lambda e: e.reciprocal(sm8, sm8),
              reads=[sm("stc", 24, 32)], writes=[sm("stc", 24, 32)])
            A("DVE", lambda e: e.tensor_tensor(
                prw8[:, :, :], prw8[:, :, :], bc8(sm8), ALU.mult),
                reads=[P8, sm("stc", 24, 32)], writes=[P8])
            A("DVE", lambda e: e.tensor_tensor(
                cw8[:, :, :], prw8[:, :, :],
                es_b[:, None, :].broadcast_to([128, HLF, E]), ALU.mult),
                reads=[P8, sm("esb")], writes=[C8], tag="cw8")
            def d1_tail(ti):
                tj = ti % 2
                tk2 = 2 + ti % 2
                A("PE", lambda e, ti_=ti, t_=tj: e.transpose(
                    tpp[t_][:, 0:E], cw8[:, ti_, :], idbf[:, :]),
                    reads=[C8, sm("idbf")], writes=[TPR[tj]])
                A("ACT", lambda e, t_=tj, ti_=ti: e.copy(
                    cwT[:, ti_ % 2, :], tpp[t_][:, 0:E]),
                  reads=[TPR[tj]], writes=[sm("cwT", ti % 2, ti % 2 + 1)])
                wb = (ti % 2) * 2
                A("PE", lambda e, bk=wb, ti_=ti: e.matmul(
                    mm[bk][:, 0:K], cwT[:, ti_ % 2, :], mix_bf[:, :],
                    start=True, stop=True),
                    reads=[sm("cwT", ti % 2, ti % 2 + 1), sm("mixbf")],
                    writes=[MMR[wb]], tag=f"wkmm{ti}")
                A("ACT", lambda e, ti_=ti, bk=wb: e.copy(
                    wk_g[:, ti_, :], mm[bk][:, 0:K]),
                    reads=[MMR[wb]], writes=[sm("wkg", ti, ti + 1)],
                    tag=f"wkev{ti}")
            for ch, (co, cwd) in enumerate(_chunks(KI, min(512, I))):
                wj = ch % 2
                kk = co // I
                WGIV = (RW, wj * NHB * 512, (wj + 1) * NHB * 512)
                WUIV = (RW, (2 + wj) * NHB * 512, (3 + wj) * NHB * 512)
                A("POOL", lambda e, jj=wj, co_=co, cw_=cwd: e.dma_start(
                    out=wg_st[jj][:, :, 0:cw_],
                    in_=Wg[:, co_:co_ + cw_].rearrange(
                        "(b p) c -> p b c", p=128)),
                    writes=[WGIV], dma=True, tag=f"ldwg{ch}")
                A("POOL", lambda e, jj=wj, co_=co, cw_=cwd: e.dma_start(
                    out=wu_st[jj][:, :, 0:cw_],
                    in_=Wu[:, co_:co_ + cw_].rearrange(
                        "(b p) c -> p b c", p=128)),
                    writes=[WUIV], dma=True, tag=f"ldwu{ch}")
                for ti in range(HLF):
                    par = ti % 2
                    bg = par * 2
                    bu = par * 2 + 1

                    def g_group(e, jj=wj, bk=bg, ti_=ti, cw_=cwd):
                        last = None
                        for hh in range(NHB):
                            last = e.matmul(
                                mm[bk][:, 0:cw_],
                                featA[:, hh, ti_ * 128:(ti_ + 1) * 128],
                                wg_st[jj][:, hh, 0:cw_],
                                start=(hh == 0), stop=(hh == NHB - 1))
                        return last
                    A("PE", g_group, reads=[WGIV, FEAT_G], writes=[MMR[bg]],
                      tag=f"gmm{ch}_{ti}")

                    def u_group(e, jj=wj, bk=bu, ti_=ti, cw_=cwd):
                        last = None
                        for hh in range(NHB):
                            last = e.matmul(
                                mm[bk][:, 0:cw_],
                                featA[:, hh, ti_ * 128:(ti_ + 1) * 128],
                                wu_st[jj][:, hh, 0:cw_],
                                start=(hh == 0), stop=(hh == NHB - 1))
                        return last
                    A("PE", u_group, reads=[WUIV, FEAT_G], writes=[MMR[bu]],
                      tag=f"umm{ch}_{ti}")
                    if ch == 0:
                        d1_tail(ti)
                    A("ACT", lambda e, bk=bg, pp=par, cw_=cwd: e.activation(
                        sgsD[pp][:, 0:cw_], mm[bk][:, 0:cw_], AF.Silu),
                        reads=[MMR[bg]], writes=[ru_iv(par, 0, 1024)],
                        tag=f"silu{ch}_{ti}")
                    A("DVE", lambda e, bk=bu, pp=par, ti_=ti, kk_=kk,
                      cw_=cwd: e.scalar_tensor_tensor(
                          tbD[pp][:, 0:cw_], sgsD[pp][:, 0:cw_],
                          wk_g[:, ti_, kk_:kk_ + 1], mm[bk][:, 0:cw_],
                          ALU.mult, ALU.mult),
                      reads=[ru_iv(par, 0, 1024), sm("wkg", ti, ti + 1),
                             MMR[bu]],
                      writes=[ru_iv(par, 1024, 2048)], tag=f"tb{ch}_{ti}")
                    ci0 = co % I
                    CIV = (RV, 2 * (ti * I + ci0), 2 * (ti * I + ci0 + cwd))
                    if kk == 0:
                        A("DVE", lambda e, pp=par, ti_=ti, ci=ci0, cw_=cwd:
                          e.tensor_copy(cacc[:, ti_, ci:ci + cw_],
                                        tbD[pp][:, 0:cw_]),
                          reads=[ru_iv(par, 1024, 2048)], writes=[CIV],
                          tag=f"cset{ch}_{ti}")
                    else:
                        A("DVE", lambda e, pp=par, ti_=ti, ci=ci0, cw_=cwd:
                          e.tensor_tensor(cacc[:, ti_, ci:ci + cw_],
                                          cacc[:, ti_, ci:ci + cw_],
                                          tbD[pp][:, 0:cw_], ALU.add),
                          reads=[ru_iv(par, 1024, 2048), CIV], writes=[CIV],
                          tag=f"cacc{ch}_{ti}")
            for ti in range(HLF):
                CAV = (RV, 2 * ti * I, 2 * (ti + 1) * I)
                A("DVE", lambda e, ti_=ti: e.tensor_copy(
                    cbf[:, :], cacc[:, ti_, :]),
                    reads=[CAV], writes=[(RV, CBO, CBO + I)], tag=f"cbf{ti}")
                for ib in range(NIB):
                    tj = ib % 4
                    A("PE", lambda e, ib_=ib, tj_=tj: e.transpose(
                        tpp[tj_][:, 0:128],
                        cbf[:, ib_ * 128:(ib_ + 1) * 128], idbf[:, :]),
                        reads=[(RV, CBO, CBO + I), sm("idbf")],
                        writes=[TPR[tj]], tag=f"ctp{ti}_{ib}")
                    A("DVE", lambda e, ib_=ib, tj_=tj, ti_=ti: e.tensor_copy(
                        cT[:, ib_, ti_ * 128:(ti_ + 1) * 128],
                        tpp[tj_][:, 0:128]),
                        reads=[TPR[tj]],
                        writes=[(RV, CTO + ib * GW + ti * 128,
                                 CTO + ib * GW + ti * 128 + 128)])
            CT_ALL = (RV, CTO, CTO + NIB * GW)
            for ch in range(NI2B):
                wj = ch % 2
                WSGIV = (RW, WSGO + wj * NHB * 128,
                         WSGO + (wj + 1) * NHB * 128)
                WSUIV = (RW, WSUO + wj * NHB * 128,
                         WSUO + (wj + 1) * NHB * 128)
                A("POOL", lambda e, jj=wj, ch_=ch: e.dma_start(
                    out=wsg_st[jj][:, :, :],
                    in_=Wsg[:, ch_ * 128:(ch_ + 1) * 128].rearrange(
                        "(b p) c -> p b c", p=128)),
                    writes=[WSGIV], dma=True, tag=f"ldwsg{ch}")
                A("POOL", lambda e, jj=wj, ch_=ch: e.dma_start(
                    out=wsu_st[jj][:, :, :],
                    in_=Wsu[:, ch_ * 128:(ch_ + 1) * 128].rearrange(
                        "(b p) c -> p b c", p=128)),
                    writes=[WSUIV], dma=True, tag=f"ldwsu{ch}")
                for ts_, (toff, tw) in enumerate(_chunks(GW, 512)):
                    par = ts_ % 2
                    bg = par * 2
                    bu = par * 2 + 1

                    def sg_group(e, jj=wj, bk=bg, to=toff, tw_=tw):
                        last = None
                        for hh in range(NHB):
                            last = e.matmul(
                                mm[bk][:, 0:tw_], wsg_st[jj][:, hh, :],
                                featA[:, hh, to:to + tw_],
                                start=(hh == 0), stop=(hh == NHB - 1))
                        return last
                    A("PE", sg_group, reads=[WSGIV, FEAT_G],
                      writes=[MMR[bg]], tag=f"sgmm{ch}_{ts_}")

                    def su_group(e, jj=wj, bk=bu, to=toff, tw_=tw):
                        last = None
                        for hh in range(NHB):
                            last = e.matmul(
                                mm[bk][:, 0:tw_], wsu_st[jj][:, hh, :],
                                featA[:, hh, to:to + tw_],
                                start=(hh == 0), stop=(hh == NHB - 1))
                        return last
                    A("PE", su_group, reads=[WSUIV, FEAT_G],
                      writes=[MMR[bu]], tag=f"summ{ch}_{ts_}")
                    A("ACT", lambda e, bk=bg, pp=par, tw_=tw: e.activation(
                        sgs2[pp][:, 0:tw_], mm[bk][:, 0:tw_], AF.Silu),
                        reads=[MMR[bg]],
                        writes=[(RV, SG2O + par * 1024,
                                 SG2O + (par + 1) * 1024)],
                        tag=f"ssil{ch}_{ts_}")
                    A("DVE", lambda e, bk=bu, pp=par, ch_=ch, to=toff,
                      tw_=tw: e.tensor_tensor(
                          hshT[:, ch_, to:to + tw_], sgs2[pp][:, 0:tw_],
                          mm[bk][:, 0:tw_], ALU.mult),
                      reads=[(RV, SG2O + par * 1024,
                              SG2O + (par + 1) * 1024), MMR[bu]],
                      writes=[(RW, ch * GW + toff, ch * GW + toff + tw)],
                      tag=f"hsh{ch}_{ts_}")
            HSH_ALL = (RW, 0, NI2B * GW)
            for hc, (ho, hw) in enumerate(_chunks(H, 512)):
                wj = hc % 2
                WDIV = (RV, wj * NIB * 512, (wj + 1) * NIB * 512)
                WSDIV = (RV, WSDO + wj * NI2B * 512,
                         WSDO + (wj + 1) * NI2B * 512)
                A("POOL", lambda e, jj=wj, ho_=ho, hw_=hw: e.dma_start(
                    out=wd_st[jj][:, :, 0:hw_],
                    in_=Wd[:, ho_:ho_ + hw_].rearrange(
                        "(b p) c -> p b c", p=128)),
                    writes=[WDIV], dma=True, tag=f"ldwd{hc}")
                A("POOL", lambda e, jj=wj, ho_=ho, hw_=hw: e.dma_start(
                    out=wsd_st[jj][:, :, 0:hw_],
                    in_=Wsd[:, ho_:ho_ + hw_].rearrange(
                        "(b p) c -> p b c", p=128)),
                    writes=[WSDIV], dma=True, tag=f"ldwsd{hc}")
                for ti in range(HLF):
                    t = gt0 + ti
                    par = ti % 2
                    bank = (hc * HLF + ti) % 4
                    A("ACT", lambda e, tt=t, pp=par, ho_=ho, hw_=hw:
                      e.dma_start(
                          out=x_sl[pp][0:128, 0:hw_],
                          in_=X[b, tt * 128:(tt + 1) * 128, ho_:ho_ + hw_]),
                      writes=[rx_iv(par, 0, 512)], dma=True)
                    A("ACT", lambda e, tt=t, pp=par, ho_=ho, hw_=hw:
                      e.dma_start(
                          out=s_sl[pp][0:128, 0:hw_],
                          in_=s_sp[b, tt * 128:(tt + 1) * 128,
                                   ho_:ho_ + hw_]),
                      reads=[(RD["d_s"], b * NT + t, b * NT + t + 1)],
                      writes=[ru_iv(par, 512, 1024)], dma=True)

                    def d_group(e, jj=wj, bk=bank, ti_=ti, hw_=hw):
                        last = None
                        for ib_ in range(NIB):
                            last = e.matmul(
                                mm[bk][:, 0:hw_],
                                cT[:, ib_, ti_ * 128:(ti_ + 1) * 128],
                                wd_st[jj][:, ib_, 0:hw_],
                                start=(ib_ == 0), stop=False)
                        for i2_ in range(NI2B):
                            last = e.matmul(
                                mm[bk][:, 0:hw_],
                                hshT[:, i2_, ti_ * 128:(ti_ + 1) * 128],
                                wsd_st[jj][:, i2_, 0:hw_],
                                start=False, stop=(i2_ == NI2B - 1))
                        return last
                    A("PE", d_group,
                      reads=[CT_ALL, WDIV, HSH_ALL, WSDIV],
                      writes=[MMR[bank]], tag=f"dmm{hc}_{ti}")
                    A("ACT", lambda e, bk=bank, pp=par, hw_=hw: e.activation(
                        gsb[pp][:, 0:hw_], mm[bk][:, 0:hw_], AF.Silu),
                        reads=[MMR[bank]], writes=[rx_iv(par, 512, 1024)],
                        tag=f"gsil{hc}_{ti}")
                    A("DVE", lambda e, pp=par, hw_=hw: e.tensor_tensor(
                        tb2[pp][:, 0:hw_], gsb[pp][:, 0:hw_],
                        s_sl[pp][0:128, 0:hw_], ALU.mult),
                        reads=[rx_iv(par, 512, 1024), ru_iv(par, 512, 1024)],
                        writes=[rx_iv(par, 1024, 1536)], tag=f"sg{hc}_{ti}")
                    A("DVE", lambda e, pp=par, hw_=hw: e.tensor_tensor(
                        ostage[pp][:, 0:hw_], tb2[pp][:, 0:hw_],
                        x_sl[pp][0:128, 0:hw_], ALU.add),
                        reads=[rx_iv(par, 1024, 1536), rx_iv(par, 0, 512)],
                        writes=[rx_iv(par, 1536, 2048)], tag=f"fin{hc}_{ti}")
                    A("ACT", lambda e, tt=t, pp=par, ho_=ho, hw_=hw:
                      e.dma_start(
                          out=OUT[b, tt * 128:(tt + 1) * 128, ho_:ho_ + hw_],
                          in_=ostage[pp][:, 0:hw_]),
                      reads=[rx_iv(par, 1536, 2048)],
                      writes=[dr("d_out", b)], dma=True,
                      tag=f"stout{hc}_{ti}")

    for _b in range(B_CORE):
        emit_batch(_b)

    with nc.Block() as block:
        s.finalize(block, st)
    return st


def kernel(**inputs):
    D = REAL_DIMS
    n_cores = 8
    x = np.ascontiguousarray(inputs["L0_attn_x"], dtype=np.float32)
    xs = np.split(x, n_cores, axis=0)
    nc = bass.Bass("TRN2", target_bir_lowering=False)
    build_program(nc, D)
    weights = {k: np.ascontiguousarray(v, dtype=np.float32)
               for k, v in inputs.items()
               if k not in ("L0_attn_x", "L0_attn_gamma", "L0_attn_beta")}
    weights["gamma_pb"] = np.ascontiguousarray(
        np.asarray(inputs["L0_attn_gamma"], dtype=np.float32)
        .reshape(-1, 128).T)
    weights["beta_pb"] = np.ascontiguousarray(
        np.asarray(inputs["L0_attn_beta"], dtype=np.float32)
        .reshape(-1, 128).T)
    in_maps = [dict(weights, L0_attn_x=xs[c]) for c in range(n_cores)]
    res = run_bass_kernel_spmd(nc, in_maps, core_ids=list(range(n_cores)))
    return np.concatenate([r["out"] for r in res.results], axis=0)


# version 10 (final)
# speedup vs baseline: 1.0242x; 1.0242x over previous
import numpy as np
from contextlib import ExitStack

import concourse.bass as bass
import concourse.mybir as mybir
from concourse.bass_utils import run_bass_kernel_spmd

F32 = mybir.dt.float32
BF16 = mybir.dt.bfloat16
AF = mybir.ActivationFunctionType
ALU = mybir.AluOpType
AXX = mybir.AxisListType.X

ENGINES = ("PE", "DVE", "ACT", "POOL", "SP")



class Region:
    __slots__ = ("name", "recs")

    def __init__(self, name):
        self.name = name
        self.recs = []


class _Op:
    __slots__ = ("engine", "emit", "deps", "is_dma", "seq", "tag",
                 "sem", "val", "need_inc", "waits")

    def __init__(self, engine, emit, is_dma, tag):
        self.engine = engine
        self.emit = emit
        self.deps = []
        self.is_dma = is_dma
        self.seq = -1
        self.tag = tag
        self.sem = None
        self.val = 0
        self.need_inc = False
        self.waits = []


class Sched:
    def __init__(self, nc, strict=True):
        self.nc = nc
        self.ops = []
        self.regions = {}
        self.strict = strict

    def region(self, name):
        r = self.regions.get(name)
        if r is None:
            r = self.regions[name] = Region(name)
        return r

    def add(self, engine, emit, reads=(), writes=(), dma=False, tag=""):
        op = _Op(engine, emit, dma, tag)
        op.seq = len(self.ops)
        dep_set = {}
        for (reg, a, b) in reads:
            for rec in reg.recs:
                if rec[0] < b and a < rec[1]:
                    w = rec[2]
                    if w is not None:
                        dep_set[w.seq] = w
                    rec[3].append(op)
        for (reg, a, b) in writes:
            keep = []
            for rec in reg.recs:
                if rec[0] < b and a < rec[1]:
                    if rec[2] is not None:
                        dep_set[rec[2].seq] = rec[2]
                    for r_ in rec[3]:
                        dep_set[r_.seq] = r_
                    if rec[0] < a:
                        keep.append([rec[0], a, rec[2], list(rec[3])])
                    if b < rec[1]:
                        keep.append([b, rec[1], rec[2], list(rec[3])])
                else:
                    keep.append(rec)
            keep.append([a, b, op, []])
            reg.recs = keep
        op.deps = [d for d in dep_set.values() if d.seq != op.seq]
        self.ops.append(op)
        return op

    def finalize(self, block, stack):
        nc = self.nc
        for op in self.ops:
            if op.is_dma:
                op.need_inc = True
        for op in self.ops:
            for d in op.deps:
                if d.engine == "PE" and op.engine == "PE" and not d.is_dma \
                        and not op.is_dma:
                    continue
                if self.strict or d.is_dma or d.engine != op.engine:
                    d.need_inc = True
        SEM_MAX = 55000
        NSLOT = 8
        counters = {}
        sems = {}
        dma_idx = {}
        throttle = {}
        for op in self.ops:
            if op.is_dma:
                i = dma_idx.get(op.engine, 0)
                dma_idx[op.engine] = i + 1
                slot = i % NSLOT
                val = 16 * (i // NSLOT + 1)
                skey = (op.engine, "q", slot)
                if skey not in sems:
                    sems[skey] = stack.enter_context(
                        nc.semaphore(f"s_{op.engine}_q{slot}"))
                op.sem = sems[skey]
                op.val = val
                if val > 16:
                    throttle[op] = (op.sem, val - 16)
                continue
            key = (op.engine, "c")
            cnt, epoch = counters.get(key, (0, 0))
            if op.need_inc and cnt + 1 > SEM_MAX:
                epoch += 1
                cnt = 0
            if op.need_inc:
                cnt += 1
            counters[key] = (cnt, epoch)
            skey = key + (epoch,)
            if op.need_inc and skey not in sems:
                sems[skey] = stack.enter_context(
                    nc.semaphore(f"s_{key[0]}_c{epoch}"))
            op.sem = sems.get(skey)
            op.val = cnt
        water = {e: {} for e in ENGINES}
        for op in self.ops:
            w = water[op.engine]
            th = throttle.get(op)
            if th is not None:
                sem, val = th
                if w.get(id(sem), -1) < val:
                    op.waits.append((sem, val))
                    w[id(sem)] = val
            for d in op.deps:
                if (d.engine == "PE" and op.engine == "PE" and not d.is_dma
                        and not op.is_dma):
                    continue
                if not d.need_inc:
                    assert d.engine == op.engine and not d.is_dma
                    continue
                if (not self.strict and d.engine == op.engine
                        and not d.is_dma and not op.is_dma):
                    continue
                sid = id(d.sem)
                if w.get(sid, -1) < d.val:
                    op.waits.append((d.sem, d.val))
                    w[sid] = d.val
        per_engine = {e: [] for e in ENGINES}
        for op in self.ops:
            per_engine[op.engine].append(op)
        for e in ENGINES:
            q = [op for op in per_engine[e] if op.is_dma]
            if q:
                finals = {}
                for d in q:
                    cur = finals.get(id(d.sem))
                    if cur is None or d.val > cur[1]:
                        finals[id(d.sem)] = (d.sem, d.val)

                def mk_drain(pairs):
                    def drain(eng):
                        for sem, val in pairs:
                            eng.wait_ge(sem, val)
                        return None
                    return drain
                op = _Op(e, mk_drain(list(finals.values())), False, "drain")
                op.seq = len(self.ops)
                self.ops.append(op)
                per_engine[e].append(op)

        def mk_body(ename):
            ops = per_engine[ename]

            def body(eng):
                for op in ops:
                    for (sem, val) in op.waits:
                        eng.wait_ge(sem, val)
                    inst = op.emit(eng)
                    if op.need_inc:
                        if inst is None:
                            raise RuntimeError(f"op {op.tag} emitted None")
                        inst.then_inc(op.sem, 16 if op.is_dma else 1)
            return body

        if per_engine["PE"]:
            block.tensor(mk_body("PE"))
        if per_engine["DVE"]:
            block.vector(mk_body("DVE"))
        if per_engine["ACT"]:
            block.scalar(mk_body("ACT"))
        if per_engine["POOL"]:
            block.gpsimd(mk_body("POOL"))
        if per_engine["SP"]:
            block.sync(mk_body("SP"))



REAL_DIMS = dict(B_CORE=2, S=2048, H=2048, KEY=2048, KI=8192, I=1024, K=8,
                 ISH=2048, E=128)


def _chunks(total, step):
    out = []
    o = 0
    while o < total:
        w = min(step, total - o)
        out.append((o, w))
        o += w
    return out


def build_program(nc, D):
    B_CORE, S, H = D["B_CORE"], D["S"], D["H"]
    KEY, KI, I, K, ISH, E = (D["KEY"], D["KI"], D["I"], D["K"], D["ISH"],
                             D["E"])
    NT = S // 128
    HLF = NT // 2
    NHB = H // 128
    NKB = KEY // 128
    NIB = I // 128
    NI2B = ISH // 128
    SH = S // 2
    GW = HLF * 128
    nbn = max(1, H // 512)
    rscale = float(1.0 / np.sqrt(np.float32(KEY)))

    X = nc.dram_tensor("L0_attn_x", [B_CORE, S, H], F32,
                       kind="ExternalInput").ap()
    Wqkv = nc.dram_tensor("L0_attn_qkv", [H, 3 * KEY], F32,
                          kind="ExternalInput").ap()
    Wo = nc.dram_tensor("L0_attn_o_proj", [KEY, H], F32,
                        kind="ExternalInput").ap()
    Gam = nc.dram_tensor("gamma_pb", [128, H // 128], F32,
                         kind="ExternalInput").ap()
    Bet = nc.dram_tensor("beta_pb", [128, H // 128], F32,
                         kind="ExternalInput").ap()
    Wr = nc.dram_tensor("L0_ffn_w_router", [H, E], F32,
                        kind="ExternalInput").ap()
    Wg = nc.dram_tensor("L0_ffn_w_gate", [H, KI], F32,
                        kind="ExternalInput").ap()
    Wu = nc.dram_tensor("L0_ffn_w_up", [H, KI], F32,
                        kind="ExternalInput").ap()
    Wd = nc.dram_tensor("L0_ffn_w_down", [I, H], F32,
                        kind="ExternalInput").ap()
    Amp = nc.dram_tensor("L0_ffn_amplitudes", [E, K, 2], F32,
                         kind="ExternalInput").ap()
    Esc = nc.dram_tensor("L0_ffn_expert_scale", [E], F32,
                         kind="ExternalInput").ap()
    Wsg = nc.dram_tensor("L0_ffn_shared_w_gate", [H, ISH], F32,
                         kind="ExternalInput").ap()
    Wsu = nc.dram_tensor("L0_ffn_shared_w_up", [H, ISH], F32,
                         kind="ExternalInput").ap()
    Wsd = nc.dram_tensor("L0_ffn_shared_w_down", [ISH, H], F32,
                         kind="ExternalInput").ap()
    OUT = nc.dram_tensor("out", [B_CORE, S, H], F32,
                         kind="ExternalOutput").ap()
    q_sp = nc.dram_tensor("q_sp", [B_CORE, NKB, 128, S], BF16,
                          kind="Internal").ap()
    kT_sp = nc.dram_tensor("kT_sp", [B_CORE, NKB, 128, S], BF16,
                           kind="Internal").ap()
    xnT_sp = nc.dram_tensor("xnT_sp", [B_CORE, NHB, 128, S], BF16,
                            kind="Internal").ap()
    u_sp = nc.dram_tensor("u_sp", [B_CORE, S, H], BF16,
                          kind="Internal").ap()
    avT_sp = nc.dram_tensor("avT_sp", [B_CORE, NKB, 128, S], BF16,
                            kind="Internal").ap()
    s_sp = nc.dram_tensor("s_sp", [B_CORE, S, H], BF16,
                          kind="Internal").ap()

    st = ExitStack()
    sbt = lambda n, shp, dt: st.enter_context(nc.sbuf_tensor(n, shp, dt))
    pst = lambda n, shp, dt: st.enter_context(nc.psum_tensor(n, shp, dt))

    VEL = max(NT * KEY,
              2 * HLF * I + I + 2 * 1024 + NIB * GW,
              2 * NIB * 512 + 2 * NI2B * 512 + NIB * GW)
    WEL = max(3 * NHB * 128 + 2 * NHB * 512 + 2 * 512,
              2 * NKB * 512 + 2 * NKB * 128,
              NKB * H,
              4 * NHB * 512,
              NI2B * GW + 4 * NHB * 128)
    FEL = max(NHB * SH,
              2 * S + 2 * S + S + 3 * NKB * 128,
              2 * NKB * 128)
    r_v = sbt("r_v", [128, VEL], BF16)
    r_w = sbt("r_w", [128, WEL], BF16)
    r_feat = sbt("r_feat", [128, FEL], BF16)
    XUW = max(H, 2048)
    r_x = sbt("r_x", [128, 2, XUW], F32)
    r_u = sbt("r_u", [128, 2, XUW], BF16)
    r_stg = sbt("r_stg", [128, 2, 512 + NHB * 128], BF16)

    idbf = sbt("idbf", [128, 128], BF16)
    idf32 = sbt("idf32", [128, 128], F32)
    causal = sbt("causal", [128, 128], F32)
    onesf = sbt("onesf", [1, 128], F32)
    stc = sbt("stc", [128, 2, 16], F32)
    zbuf = sbt("zbuf", [128, max(H, 512)], BF16)
    bnst = sbt("bnst", [128, 2, nbn, 6], F32)
    ag12 = sbt("ag12", [128, 2, 4], F32)
    cc_ = sbt("cc_", [128, 4], F32)
    gam_sb = sbt("gam_sb", [128, NHB], F32)
    bet_sb = sbt("bet_sb", [128, NHB], F32)
    wr_sb = sbt("wr_sb", [128, NHB, E], BF16)
    amp_sb = sbt("amp_sb", [128, 2 * K], F32)
    mixp = sbt("mixp", [128, K], F32)
    mix_bf = sbt("mix_bf", [128, K], BF16)
    es_row = sbt("es_row", [1, E], F32)
    es_b = sbt("es_b", [128, E], F32)
    rtr = sbt("rtr", [128, E], F32)
    cwT = sbt("cwT", [128, 2, E], BF16)
    wk_g = sbt("wk_g", [128, HLF, K], F32)
    nmx = sbt("nmx", [128, 1], F32)
    ssm = sbt("ssm", [128, 1], F32)
    rsb = sbt("rsb", [128, 1], F32)

    mm = [pst(f"mm{i}", [128, 512], F32) for i in range(4)]
    tpp = [pst(f"tp{i}", [128, 1024], BF16) for i in range(4)]

    s = Sched(nc)

    def reg(n):
        return s.region(n)

    RV, RW, RF = reg("v"), reg("w"), reg("feat")
    RX, RU, RS = reg("x"), reg("u"), reg("stg")
    SMALLS = ("idbf idf32 causal onesf statT bcb stc bnst ag12 cc gam bet "
              "wr amp mixp mixt mixbf esrow esb rtr prb prw eqm mskp cwbf "
              "cwT wkg zbuf nmx ssm rsb").split()
    RSM = {n: reg(n) for n in SMALLS}
    PS = {n: reg(n) for n in
          ("mm0", "mm1", "mm2", "mm3", "tp0", "tp1", "tp2", "tp3")}
    RD = {n: reg(n) for n in
          ("d_q", "d_k", "d_xnT", "d_u", "d_avT", "d_s", "d_out")}
    MMR = [(PS[f"mm{i}"], 0, 1) for i in range(4)]
    TPR = [(PS[f"tp{i}"], 0, 1) for i in range(4)]

    def sm(n, a=0, b=1 << 30):
        return (RSM[n], a, b)

    def dr(n, b_):
        return (RD[n], b_, b_ + 1)

    A = s.add

    vres = r_v[:, 0:NT * KEY].rearrange("p (t d) -> p t d", d=KEY)
    featA = r_feat[:, 0:NHB * SH].rearrange("p (h t) -> p h t", t=SH)
    wqkv_st = [r_w[:, j * NHB * 128:(j + 1) * NHB * 128]
               .rearrange("p (h c) -> p h c", c=128) for j in range(3)]
    VSTO = 3 * NHB * 128
    v_st = [r_w[:, VSTO + j * NHB * 512:VSTO + (j + 1) * NHB * 512]
            .rearrange("p (h c) -> p h c", c=512) for j in range(2)]
    KSTO = VSTO + 2 * NHB * 512
    kstage = [r_w[:, KSTO + j * 512:KSTO + (j + 1) * 512] for j in range(2)]
    k_b = [r_w[:, j * NKB * 512:(j + 1) * NKB * 512]
           .rearrange("p (d c) -> p d c", c=512) for j in range(2)]
    AVSO = 2 * NKB * 512
    wo_full = r_w[:, 0:NKB * H].rearrange("p (d c) -> p d c", c=H)
    scores = r_feat[:, 0:2 * S].bitcast(F32)
    pbuf = r_feat[:, 2 * S:4 * S].bitcast(F32)
    probs_bf = r_feat[:, 4 * S:5 * S]
    probsT = r_feat[:, 5 * S:5 * S + NKB * 128].rearrange(
        "p (d t) -> p d t", t=128)
    QTO = 5 * S + NKB * 128
    qtile = [r_feat[:, QTO + j * NKB * 128:QTO + (j + 1) * NKB * 128]
             .rearrange("p (d t) -> p d t", t=128) for j in range(2)]
    stg_flat = r_stg[:, :, :].rearrange("p a b -> p (a b)")
    avstage = [stg_flat[:, j * NKB * 128:(j + 1) * NKB * 128]
               .rearrange("p (d c) -> p d c", c=128) for j in range(2)]
    av_b = [stg_flat[:, j * NKB * 128:(j + 1) * NKB * 128]
            .rearrange("p (d t) -> p d t", t=128) for j in range(2)]
    rtr8 = stg_flat[:, 0:2 * HLF * E].bitcast(F32).rearrange(
        "p (t e) -> p t e", e=E)
    prw8 = stg_flat[:, 2 * HLF * E:4 * HLF * E].bitcast(F32).rearrange(
        "p (t e) -> p t e", e=E)
    cw8 = stg_flat[:, 4 * HLF * E:5 * HLF * E].rearrange(
        "p (t e) -> p t e", e=E)
    wg_st = [r_w[:, j * NHB * 512:(j + 1) * NHB * 512]
             .rearrange("p (h c) -> p h c", c=512) for j in range(2)]
    wu_st = [r_w[:, (2 + j) * NHB * 512:(3 + j) * NHB * 512]
             .rearrange("p (h c) -> p h c", c=512) for j in range(2)]
    hshT = r_w[:, 0:NI2B * GW].rearrange("p (d t) -> p d t", t=GW)
    WSGO = NI2B * GW
    wsg_st = [r_w[:, WSGO + j * NHB * 128:WSGO + (j + 1) * NHB * 128]
              .rearrange("p (h c) -> p h c", c=128) for j in range(2)]
    WSUO = WSGO + 2 * NHB * 128
    wsu_st = [r_w[:, WSUO + j * NHB * 128:WSUO + (j + 1) * NHB * 128]
              .rearrange("p (h c) -> p h c", c=128) for j in range(2)]
    cacc = r_v[:, 0:2 * HLF * I].bitcast(F32).rearrange(
        "p (t i) -> p t i", i=I)
    CBO = 2 * HLF * I
    cbf = r_v[:, CBO:CBO + I]
    SG2O = CBO + I
    sgs2 = [r_v[:, SG2O + j * 1024:SG2O + (j + 1) * 1024].bitcast(F32)
            for j in range(2)]
    CTO = VEL - NIB * GW
    cT = r_v[:, CTO:CTO + NIB * GW].rearrange("p (d t) -> p d t", t=GW)
    wd_st = [r_v[:, j * NIB * 512:(j + 1) * NIB * 512]
             .rearrange("p (d c) -> p d c", c=512) for j in range(2)]
    WSDO = 2 * NIB * 512
    wsd_st = [r_v[:, WSDO + j * NI2B * 512:WSDO + (j + 1) * NI2B * 512]
              .rearrange("p (d c) -> p d c", c=512) for j in range(2)]
    assert WSDO + 2 * NI2B * 512 <= CTO, "wd/wsd overlap cT"
    u_sl = [r_u[:, j, 0:512] for j in range(2)]
    s_sl = [r_u[:, j, 512:1024] for j in range(2)]
    sstage = [r_u[:, j, 1024:1536] for j in range(2)]
    sgsD = [r_u[:, j, 0:1024].bitcast(F32) for j in range(2)]
    tbD = [r_u[:, j, 1024:2048].bitcast(F32) for j in range(2)]
    x_sl = [r_x[:, j, 0:512] for j in range(2)]
    gsb = [r_x[:, j, 512:1024] for j in range(2)]
    tb2 = [r_x[:, j, 1024:1536] for j in range(2)]
    ostage = [r_x[:, j, 1536:2048] for j in range(2)]
    qstage = [r_stg[:, j, 0:512] for j in range(2)]
    xnstage = [r_stg[:, j, 512:512 + NHB * 128].rearrange(
        "p (h c) -> p h c", c=128) for j in range(2)]

    def ru_iv(par, a, b):
        return (RU, par * XUW + a, par * XUW + b)

    def rx_iv(par, a, b):
        return (RX, par * XUW + a, par * XUW + b)

    def rstg_iv(par, a, b):
        W2 = 512 + NHB * 128
        return (RS, par * W2 + a, par * W2 + b)

    A("POOL", lambda e: e.memset(cc_[:, 0:1], 1e-5), writes=[sm("cc", 0, 1)])
    A("POOL", lambda e: e.memset(cc_[:, 1:2], -0.5), writes=[sm("cc", 1, 2)])
    A("POOL", lambda e: e.memset(cc_[:, 2:3], 1e-10), writes=[sm("cc", 2, 3)])
    A("POOL", lambda e: e.memset(cc_[:, 3:4], 1.00001),
      writes=[sm("cc", 3, 4)])
    A("POOL", lambda e: e.memset(idbf[:, :], 0.0), writes=[sm("idbf")])
    A("POOL", lambda e: e.affine_select(
        out=idbf[:, :], in_=idbf[:, :], compare_op=ALU.not_equal, fill=1.0,
        base=0, pattern=[[-1, 128]], channel_multiplier=1),
        reads=[sm("idbf")], writes=[sm("idbf")])
    A("POOL", lambda e: e.memset(idf32[:, :], 0.0), writes=[sm("idf32")])
    A("POOL", lambda e: e.affine_select(
        out=idf32[:, :], in_=idf32[:, :], compare_op=ALU.not_equal, fill=1.0,
        base=0, pattern=[[-1, 128]], channel_multiplier=1),
        reads=[sm("idf32")], writes=[sm("idf32")])
    A("POOL", lambda e: e.memset(causal[:, :], 0.0), writes=[sm("causal")])
    A("POOL", lambda e: e.affine_select(
        out=causal[:, :], in_=causal[:, :], compare_op=ALU.is_ge, fill=-1e30,
        base=0, pattern=[[-1, 128]], channel_multiplier=1),
        reads=[sm("causal")], writes=[sm("causal")])
    A("POOL", lambda e: e.memset(onesf[:, :], 1.0), writes=[sm("onesf")])
    A("POOL", lambda e: e.dma_start(
        out=wr_sb[:, :, :], in_=Wr.rearrange("(b p) c -> p b c", p=128)),
        writes=[sm("wr")], dma=True)
    A("SP", lambda e: e.dma_start(out=gam_sb[:, :], in_=Gam[:, :]),
      writes=[sm("gam")], dma=True)
    A("SP", lambda e: e.dma_start(out=bet_sb[:, :], in_=Bet[:, :]),
      writes=[sm("bet")], dma=True)
    A("SP", lambda e: e.dma_start(
        out=amp_sb[:, :], in_=Amp.rearrange("e k t -> e (k t)")),
        writes=[sm("amp")], dma=True)
    A("SP", lambda e: e.dma_start(out=es_row[0:1, :], in_=Esc[None, :]),
      writes=[sm("esrow")], dma=True)
    A("DVE", lambda e: e.tensor_mul(mixp[:, :], amp_sb[:, 0:2 * K:2],
                                    amp_sb[:, 0:2 * K:2]),
      reads=[sm("amp")], writes=[sm("mixp")])
    A("DVE", lambda e: e.tensor_mul(rtr[:, 0:K], amp_sb[:, 1:2 * K:2],
                                    amp_sb[:, 1:2 * K:2]),
      reads=[sm("amp")], writes=[sm("rtr")])
    A("DVE", lambda e: e.tensor_add(mixp[:, :], mixp[:, :], rtr[:, 0:K]),
      reads=[sm("mixp"), sm("rtr")], writes=[sm("mixp")])
    A("DVE", lambda e: e.reduce_sum(stc[:, 0, 0:1], mixp[:, :], axis=AXX),
      reads=[sm("mixp")], writes=[sm("stc", 0, 1)])
    A("DVE", lambda e: e.tensor_scalar_add(stc[:, 0, 0:1], stc[:, 0, 0:1], 1e-8),
      reads=[sm("stc", 0, 1)], writes=[sm("stc", 0, 1)])
    A("DVE", lambda e: e.reciprocal(stc[:, 0, 1:2], stc[:, 0, 0:1]),
      reads=[sm("stc", 0, 1)], writes=[sm("stc", 1, 2)])
    A("DVE", lambda e: e.tensor_scalar(mix_bf[:, :], mixp[:, :], stc[:, 0, 1:2],
                                       None, ALU.mult),
      reads=[sm("mixp"), sm("stc", 1, 2)], writes=[sm("mixbf")])
    A("PE", lambda e: e.matmul(mm[0][0:128, 0:E], onesf[0:1, 0:128],
                               es_row[0:1, :], start=True, stop=True),
      reads=[sm("onesf"), sm("esrow")], writes=[MMR[0]])
    A("DVE", lambda e: e.tensor_copy(es_b[:, :], mm[0][0:128, 0:E]),
      reads=[MMR[0]], writes=[sm("esb")])

    def emit_batch(b):
        zb_v = [r_x[:, j, 0:(H + 1) // 2].bitcast(BF16) for j in range(2)]

        def a1_head(t):
            par = t % 2
            A("SP", lambda e, tt=t, pp=par: e.dma_start(
                out=r_x[:, pp, 0:H],
                in_=X[b, tt * 128:(tt + 1) * 128, :]),
                writes=[rx_iv(par, 0, H)], dma=True, tag=f"ldx{t}")
            for c, (off, wd_) in enumerate(_chunks(H, 512)):
                A("DVE", lambda e, pp=par, cc=c, oo=off, ww=wd_:
                  e.bn_stats(bnst[:, pp, cc, :], r_x[:, pp, oo:oo + ww]),
                  reads=[rx_iv(par, 0, H)],
                  writes=[sm("bnst", par * nbn + c, par * nbn + c + 1)])
            A("DVE", lambda e, pp=par: e.bn_aggr(ag12[:, pp, 0:2],
                                                 bnst[:, pp, :, :]),
              reads=[sm("bnst", par * nbn, par * nbn + nbn)],
              writes=[sm("ag12", par * 4, par * 4 + 2)])
            A("ACT", lambda e, pp=par: e.activation(
                stc[:, pp, 2:3], ag12[:, pp, 1:2], AF.Ln,
                bias=cc_[:, 0:1]),
                reads=[sm("ag12", par * 4, par * 4 + 2), sm("cc", 0, 1)],
                writes=[sm("stc", par * 16 + 2, par * 16 + 3)])
            A("ACT", lambda e, pp=par: e.activation(
                stc[:, pp, 3:4], stc[:, pp, 2:3], AF.Exp,
                scale=cc_[:, 1:2]),
                reads=[sm("stc", par * 16 + 2, par * 16 + 3),
                       sm("cc", 1, 2)],
                writes=[sm("stc", par * 16 + 3, par * 16 + 4)])
            A("DVE", lambda e, pp=par: e.tensor_scalar(
                r_u[:, pp, 0:H], r_x[:, pp, 0:H], ag12[:, pp, 0:1],
                stc[:, pp, 3:4], ALU.subtract, ALU.mult),
                reads=[rx_iv(par, 0, H), sm("ag12", par * 4, par * 4 + 2),
                       sm("stc", par * 16 + 3, par * 16 + 4)],
                writes=[ru_iv(par, 0, H)], tag=f"xn{t}")
            A("SP", lambda e, tt=t, pp=par: e.dma_start(
                out=u_sp[b, tt * 128:(tt + 1) * 128, :],
                in_=r_u[:, pp, 0:H]),
                reads=[ru_iv(par, 0, H)],
                writes=[(RD["d_u"], b * NT + t, b * NT + t + 1)],
                dma=True, tag=f"spu{t}")
            A("ACT", lambda e, pp=par: e.activation(
                stc[:, pp, 6:7], ag12[:, pp, 1:2], AF.Ln,
                bias=cc_[:, 2:3], scale=cc_[:, 3:4]),
                reads=[sm("ag12", par * 4, par * 4 + 2), sm("cc", 2, 4)],
                writes=[sm("stc", par * 16 + 6, par * 16 + 7)])
            A("ACT", lambda e, pp=par: e.activation(
                stc[:, pp, 7:8], stc[:, pp, 6:7], AF.Exp,
                scale=cc_[:, 1:2]),
                reads=[sm("stc", par * 16 + 6, par * 16 + 7),
                       sm("cc", 1, 2)],
                writes=[sm("stc", par * 16 + 7, par * 16 + 8)])
            A("DVE", lambda e, pp=par: e.tensor_scalar(
                zb_v[pp][:, 0:H], r_x[:, pp, 0:H], ag12[:, pp, 0:1],
                stc[:, pp, 7:8], ALU.subtract, ALU.mult),
                reads=[rx_iv(par, 0, H),
                       sm("ag12", par * 4, par * 4 + 2),
                       sm("stc", par * 16 + 7, par * 16 + 8)],
                writes=[rx_iv(par, 0, (H + 1) // 2)], tag=f"z{t}")

        def a1_tail(t):
            par = t % 2
            ti = t % HLF
            for hb in range(NHB):
                txn = hb % 2
                tz = 2 + hb % 2
                A("PE", lambda e, pp=par, hh=hb, tj=txn: e.transpose(
                    tpp[tj][:, 0:128],
                    r_u[:, pp, hh * 128:(hh + 1) * 128], idbf[:, :]),
                    reads=[ru_iv(par, 0, H), sm("idbf")],
                    writes=[TPR[txn]], tag=f"utp{t}_{hb}")
                A("ACT", lambda e, pp=par, hh=hb, tj=txn:
                  e.copy(xnstage[pp][:, hh, :], tpp[tj][:, 0:128]),
                  reads=[TPR[txn]],
                  writes=[rstg_iv(par, 512 + hb * 128,
                                  512 + hb * 128 + 128)])
                A("PE", lambda e, pp=par, hh=hb, tj=tz: e.transpose(
                    tpp[tj][:, 0:128],
                    zb_v[pp][:, hh * 128:(hh + 1) * 128], idbf[:, :]),
                    reads=[rx_iv(par, 0, (H + 1) // 2), sm("idbf")],
                    writes=[TPR[tz]], tag=f"ztp{t}_{hb}")
                A("DVE", lambda e, hh=hb, ti_=ti, tj=tz: e.tensor_scalar(
                    featA[:, hh, ti_ * 128:(ti_ + 1) * 128],
                    tpp[tj][:, 0:128], gam_sb[:, hh:hh + 1],
                    bet_sb[:, hh:hh + 1], ALU.mult, ALU.add),
                    reads=[TPR[tz], sm("gam"), sm("bet")],
                    writes=[(RF, hb * SH + ti * 128,
                             hb * SH + ti * 128 + 128)],
                    tag=f"xnorm{t}_{hb}")
            A("SP", lambda e, tt=t, pp=par: e.dma_start(
                out=xnT_sp[b, :, :, tt * 128:(tt + 1) * 128].rearrange(
                    "h p t -> p h t"),
                in_=xnstage[pp][:, :, :]),
                reads=[rstg_iv(par, 512, 512 + NHB * 128)],
                writes=[(RD["d_xnT"], b * 2 + t // HLF,
                         b * 2 + t // HLF + 1)],
                dma=True, tag=f"spxnT{t}")

        for half in range(2):
            t0 = half * HLF
            a1_head(t0)
            for ti in range(HLF):
                if ti + 1 < HLF:
                    a1_head(t0 + ti + 1)
                a1_tail(t0 + ti)
            FEAT_ALL = (RF, 0, NHB * SH)
            rot = 0
            for cc in range(2 * NKB):
                wj = cc % 3
                c0 = cc * 128
                WIV = (RW, wj * NHB * 128, (wj + 1) * NHB * 128)
                A("POOL", lambda e, jj=wj, c_=c0: e.dma_start(
                    out=wqkv_st[jj][:, :, :],
                    in_=Wqkv[:, c_:c_ + 128].rearrange(
                        "(b p) c -> p b c", p=128)),
                    writes=[WIV], dma=True, tag=f"ldqkv{cc}")
                for ts_, (toff, tw) in enumerate(_chunks(SH, 512)):
                    bank = rot % 4
                    rot += 1

                    def qk_group(e, jj=wj, bk=bank, to=toff, tw_=tw):
                        last = None
                        for hh in range(NHB):
                            last = e.matmul(
                                mm[bk][:, 0:tw_], wqkv_st[jj][:, hh, :],
                                featA[:, hh, to:to + tw_],
                                start=(hh == 0), stop=(hh == NHB - 1))
                        return last
                    A("PE", qk_group, reads=[WIV, FEAT_ALL],
                      writes=[MMR[bank]], tag=f"qkmm{cc}_{ts_}")
                    gtoff = half * SH + toff
                    qp = rot % 2
                    if cc < NKB:
                        A("DVE", lambda e, bk=bank, qq=qp, tw_=tw:
                          e.tensor_copy(qstage[qq][:, 0:tw_],
                                        mm[bk][:, 0:tw_]),
                          reads=[MMR[bank]], writes=[rstg_iv(qp, 0, 512)])
                        A("SP", lambda e, c_=cc, qq=qp, go=gtoff, tw_=tw:
                          e.dma_start(out=q_sp[b, c_, :, go:go + tw_],
                                      in_=qstage[qq][:, 0:tw_]),
                          reads=[rstg_iv(qp, 0, 512)],
                          writes=[(RD["d_q"], b * 2 + half,
                                   b * 2 + half + 1)], dma=True)
                    else:
                        dk = cc - NKB
                        A("DVE", lambda e, bk=bank, qq=qp, tw_=tw:
                          e.tensor_copy(kstage[qq][:, 0:tw_],
                                        mm[bk][:, 0:tw_]),
                          reads=[MMR[bank]],
                          writes=[(RW, KSTO + qp * 512,
                                   KSTO + qp * 512 + 512)])
                        A("SP", lambda e, d_=dk, qq=qp, go=gtoff, tw_=tw:
                          e.dma_start(out=kT_sp[b, d_, :, go:go + tw_],
                                      in_=kstage[qq][:, 0:tw_]),
                          reads=[(RW, KSTO + qp * 512, KSTO + qp * 512 + 512)],
                          writes=[(RD["d_k"], b * 2 + half,
                                   b * 2 + half + 1)], dma=True)
            for vc, (vo, vw) in enumerate(_chunks(KEY, 512)):
                vj = vc % 2
                VIV = (RW, VSTO + vj * NHB * 512, VSTO + (vj + 1) * NHB * 512)
                A("POOL", lambda e, jj=vj, vo_=vo, vw_=vw: e.dma_start(
                    out=v_st[jj][:, :, 0:vw_],
                    in_=Wqkv[:, 2 * KEY + vo_:2 * KEY + vo_ + vw_].rearrange(
                        "(b p) c -> p b c", p=128)),
                    writes=[VIV], dma=True, tag=f"ldv{vc}")
                for ti in range(HLF):
                    t = t0 + ti
                    bank = rot % 4
                    rot += 1

                    def v_group(e, jj=vj, bk=bank, ti_=ti, vw_=vw):
                        last = None
                        for hh in range(NHB):
                            last = e.matmul(
                                mm[bk][:, 0:vw_],
                                featA[:, hh, ti_ * 128:(ti_ + 1) * 128],
                                v_st[jj][:, hh, 0:vw_],
                                start=(hh == 0), stop=(hh == NHB - 1))
                        return last
                    A("PE", v_group, reads=[VIV, FEAT_ALL],
                      writes=[MMR[bank]], tag=f"vmm{vc}_{t}")
                    A("DVE", lambda e, bk=bank, tt=t, vo_=vo, vw_=vw:
                      e.tensor_copy(vres[:, tt, vo_:vo_ + vw_],
                                    mm[bk][:, 0:vw_]),
                      reads=[MMR[bank]],
                      writes=[(RV, t * KEY + vo, t * KEY + vo + vw)])

        V_ALL = (RV, 0, NT * KEY)
        scoresP = [scores, pbuf]

        def b_scores_pair(p2):
            iA, iB = 2 * p2, 2 * p2 + 1
            LB = (iB + 1) * 128
            for i_, qp in ((iA, 0), (iB, 1)):
                QIV = (RF, QTO + qp * NKB * 128, QTO + (qp + 1) * NKB * 128)
                A("SP", lambda e, ii=i_, qq=qp: e.dma_start(
                    out=qtile[qq][:, :, :],
                    in_=q_sp[b, :, :, ii * 128:(ii + 1) * 128].rearrange(
                        "d p t -> p d t")),
                    reads=[(RD["d_q"], b * 2 + i_ // HLF,
                            b * 2 + i_ // HLF + 1)],
                    writes=[QIV], dma=True, tag=f"ldq{i_}")
            for c, (soff, swd) in enumerate(_chunks(LB, 512)):
                kj = c % 2
                KIV = (RW, kj * NKB * 512, (kj + 1) * NKB * 512)
                A("POOL" if c % 2 == 0 else "SP",
                  lambda e, jj=kj, so=soff, sw=swd: e.dma_start(
                      out=k_b[jj][:, :, 0:sw],
                      in_=kT_sp[b, :, :, so:so + sw].rearrange(
                          "d p t -> p d t")),
                  reads=[(RD["d_k"], b * 2 + soff // SH,
                          b * 2 + (soff + swd - 1) // SH + 1)],
                  writes=[KIV], dma=True, tag=f"ldk{p2}_{c}")
                for i_, qp in ((iA, 0), (iB, 1)):
                    L = (i_ + 1) * 128
                    if soff >= L:
                        continue
                    sw = min(swd, L - soff)
                    bank = qp
                    QIV = (RF, QTO + qp * NKB * 128,
                           QTO + (qp + 1) * NKB * 128)

                    def sc_group(e, jj=kj, bk=bank, sw_=sw, qq=qp):
                        last = None
                        for dk_ in range(NKB):
                            last = e.matmul(
                                mm[bk][:, 0:sw_], qtile[qq][:, dk_, :],
                                k_b[jj][:, dk_, 0:sw_],
                                start=(dk_ == 0), stop=(dk_ == NKB - 1))
                        return last
                    A("PE", sc_group, reads=[QIV, KIV], writes=[MMR[bank]],
                      tag=f"sc{i_}_{c}")
                    sbuf_sc = scoresP[qp]
                    SCV = (RF, qp * 2 * S, (qp + 1) * 2 * S)
                    dstart = L - 128
                    if soff + sw <= dstart:
                        A("DVE", lambda e, bk=bank, so=soff, sw_=sw, sb=sbuf_sc:
                          e.tensor_scalar(sb[:, so:so + sw_],
                                          mm[bk][:, 0:sw_], rscale, None,
                                          ALU.mult),
                          reads=[MMR[bank]], writes=[SCV],
                          tag=f"scev{i_}_{c}")
                    else:
                        w0 = dstart - soff
                        if w0 > 0:
                            A("DVE", lambda e, bk=bank, so=soff, w_=w0,
                              sb=sbuf_sc:
                              e.tensor_scalar(sb[:, so:so + w_],
                                              mm[bk][:, 0:w_], rscale, None,
                                              ALU.mult),
                              reads=[MMR[bank]], writes=[SCV])
                        A("DVE", lambda e, bk=bank, so=soff, w_=w0,
                          sb=sbuf_sc:
                          e.scalar_tensor_tensor(
                              sb[:, so + w_:so + w_ + 128],
                              mm[bk][:, w_:w_ + 128], rscale, causal[:, :],
                              ALU.mult, ALU.add),
                          reads=[MMR[bank], sm("causal")], writes=[SCV],
                          tag=f"scdg{i_}")

        def b_softmax(i):
            L = (i + 1) * 128
            qp = i % 2
            sb = scoresP[qp]
            SCV = (RF, qp * 2 * S, (qp + 1) * 2 * S)
            A("DVE", lambda e, LL=L, sb_=sb: e.reduce_max(
                nmx[:, :], sb_[:, 0:LL], axis=AXX, negate=True),
                reads=[SCV], writes=[sm("nmx")])
            A("ACT", lambda e, LL=L, sb_=sb: e.activation(
                sb_[:, 0:LL], sb_[:, 0:LL], AF.Exp, bias=nmx[:, 0:1],
                accum_out=ssm[:, :]),
                reads=[SCV, sm("nmx")],
                writes=[SCV, sm("ssm")], tag=f"exp{i}")
            A("DVE", lambda e: e.reciprocal(rsb[:, :], ssm[:, :]),
              reads=[sm("ssm")], writes=[sm("rsb")])
            A("DVE", lambda e, LL=L, sb_=sb: e.tensor_scalar(
                probs_bf[:, 0:LL], sb_[:, 0:LL], rsb[:, 0:1], None,
                ALU.mult),
                reads=[SCV, sm("rsb")],
                writes=[(RF, 4 * S, 5 * S)], tag=f"pn{i}")

        def b_probsT(i):
            for tkb in range(i + 1):
                tj = tkb % 4
                A("PE", lambda e, tk=tkb, tj_=tj: e.transpose(
                    tpp[tj_][:, 0:128],
                    probs_bf[:, tk * 128:(tk + 1) * 128], idbf[:, :]),
                    reads=[(RF, 4 * S, 5 * S), sm("idbf")],
                    writes=[TPR[tj]], tag=f"ptp{i}_{tkb}")
                A("ACT", lambda e, tk=tkb, tj_=tj: e.copy(
                    probsT[:, tk, :], tpp[tj_][:, 0:128]),
                    reads=[TPR[tj]],
                    writes=[(RF, 5 * S + tkb * 128, 5 * S + tkb * 128 + 128)])

        def b_av(i):
            PT = (RF, 5 * S, 5 * S + (i + 1) * 128)
            apar = i % 2
            for dc in range(NKB):
                bank = 2 + dc % 2

                def av_group(e, dc_=dc, bk=bank, ii=i):
                    last = None
                    for tk in range(ii + 1):
                        last = e.matmul(
                            mm[bk][:, 0:128],
                            vres[:, tk, dc_ * 128:(dc_ + 1) * 128],
                            probsT[:, tk, :],
                            start=(tk == 0), stop=(tk == ii))
                    return last
                A("PE", av_group, reads=[V_ALL, PT], writes=[MMR[bank]],
                  tag=f"av{i}_{dc}")
                A("DVE", lambda e, dc_=dc, bk=bank, ap=apar: e.tensor_copy(
                    avstage[ap][:, dc_, :], mm[bk][:, 0:128]),
                    reads=[MMR[bank]],
                    writes=[(RS, apar * NKB * 128 + dc * 128,
                             apar * NKB * 128 + dc * 128 + 128)])
            A("SP", lambda e, ii=i, ap=apar: e.dma_start(
                out=avT_sp[b, :, :, ii * 128:(ii + 1) * 128].rearrange(
                    "d p t -> p d t"),
                in_=avstage[ap][:, :, :]),
                reads=[(RS, apar * NKB * 128, (apar + 1) * NKB * 128)],
                writes=[(RD["d_avT"], b * NT + i, b * NT + i + 1)],
                dma=True, tag=f"spav{i}")

        prev = None
        for p2 in range(NT // 2):
            iA, iB = 2 * p2, 2 * p2 + 1
            b_scores_pair(p2)
            b_softmax(iA)
            if prev is not None:
                b_av(prev)
            b_probsT(iA)
            b_softmax(iB)
            b_av(iA)
            b_probsT(iB)
            prev = iB
            if p2 == NT // 2 - 1:
                for hc_, (ho, hw) in enumerate(_chunks(H, 512)):
                    A("POOL", lambda e, ho_=ho, hw_=hw: e.dma_start(
                        out=wo_full[:, :, ho_:ho_ + hw_],
                        in_=Wo[:, ho_:ho_ + hw_].rearrange(
                            "(b p) c -> p b c", p=128)),
                        writes=[(RW, 0, NKB * H)], dma=True,
                        tag=f"ldwo{hc_}")
        b_av(prev)

        WOIV = (RW, 0, NKB * H)
        u_full = [r_u[:, j, 0:H] for j in range(2)]
        s_full = [r_x[:, j, 512:512 + (H + 1) // 2].bitcast(BF16)
                  for j in range(2)]
        for ti in range(NT):
            par = ti % 2
            AIV = (RS, par * NKB * 128, (par + 1) * NKB * 128)
            A("SP", lambda e, tt=ti, pp=par: e.dma_start(
                out=av_b[pp][:, :, :],
                in_=avT_sp[b, :, :, tt * 128:(tt + 1) * 128].rearrange(
                    "d p t -> p d t")),
                reads=[(RD["d_avT"], b * NT + ti, b * NT + ti + 1)],
                writes=[AIV], dma=True, tag=f"ldav{ti}")
            A("ACT", lambda e, tt=ti, pp=par: e.dma_start(
                out=u_full[pp][0:128, :],
                in_=u_sp[b, tt * 128:(tt + 1) * 128, :]),
                reads=[(RD["d_u"], b * NT + ti, b * NT + ti + 1)],
                writes=[ru_iv(par, 0, H)], dma=True)
            for hc, (ho, hw) in enumerate(_chunks(H, 512)):
                bank = (ti * len(_chunks(H, 512)) + hc) % 4

                def o_group(e, tt=ti, pp=par, bk=bank, ho_=ho, hw_=hw):
                    last = None
                    for dk_ in range(NKB):
                        last = e.matmul(
                            mm[bk][:, 0:hw_], av_b[pp][:, dk_, :],
                            wo_full[:, dk_, ho_:ho_ + hw_],
                            start=(dk_ == 0), stop=(dk_ == NKB - 1))
                    return last
                A("PE", o_group, reads=[AIV, WOIV], writes=[MMR[bank]],
                  tag=f"omm{ti}_{hc}")
                A("DVE", lambda e, bk=bank, pp=par, ho_=ho, hw_=hw:
                  e.tensor_tensor(
                      s_full[pp][0:128, ho_:ho_ + hw_], mm[bk][:, 0:hw_],
                      u_full[pp][0:128, ho_:ho_ + hw_], ALU.add),
                  reads=[MMR[bank], ru_iv(par, 0, H)],
                  writes=[rx_iv(par, 512, 512 + (H + 1) // 2)],
                  tag=f"sev{ti}_{hc}")
            A("ACT", lambda e, tt=ti, pp=par: e.dma_start(
                out=s_sp[b, tt * 128:(tt + 1) * 128, :],
                in_=s_full[pp][0:128, :]),
                reads=[rx_iv(par, 512, 512 + (H + 1) // 2)],
                writes=[(RD["d_s"], b * NT + ti, b * NT + ti + 1)],
                dma=True)

        for g in range(2):
            gt0 = g * HLF
            FEAT_G = (RF, 0, NHB * SH)
            A("ACT", lambda e, gg=g: e.dma_start(
                out=featA[:, :, 0:GW],
                in_=xnT_sp[b, :, :, gg * GW:(gg + 1) * GW].rearrange(
                    "h p t -> p h t")),
                reads=[(RD["d_xnT"], b * 2 + g, b * 2 + g + 1)],
                writes=[FEAT_G], dma=True, tag=f"ldxnT{g}")
            R8 = (RS, 0, 2 * HLF * E)
            P8 = (RS, 2 * HLF * E, 4 * HLF * E)
            C8 = (RS, 4 * HLF * E, 5 * HLF * E)
            nm8 = stc[:, 0, 8:8 + HLF]
            sm8 = stc[:, 1, 8:8 + HLF]

            def bc8(ap2d):
                return ap2d[:, :, None].broadcast_to([128, HLF, E])
            for ti in range(HLF):
                bank = ti % 4

                def r_group(e, ti_=ti, bk=bank):
                    last = None
                    for hh in range(NHB):
                        last = e.matmul(
                            mm[bk][:, 0:E],
                            featA[:, hh, ti_ * 128:(ti_ + 1) * 128],
                            wr_sb[:, hh, :], start=(hh == 0),
                            stop=(hh == NHB - 1))
                    return last
                A("PE", r_group, reads=[FEAT_G, sm("wr")],
                  writes=[MMR[bank]], tag=f"rtr{ti}")
                A("ACT", lambda e, ti_=ti, bk=bank: e.copy(
                    rtr8[:, ti_, :], mm[bk][:, 0:E]),
                    reads=[MMR[bank]], writes=[R8], tag=f"rtrev{ti}")
            A("DVE", lambda e: e.reduce_max(nm8, rtr8[:, :, :], axis=AXX),
              reads=[R8], writes=[sm("stc", 8, 16)])
            A("DVE", lambda e: e.tensor_tensor(
                rtr8[:, :, :], rtr8[:, :, :], bc8(nm8), ALU.subtract),
                reads=[R8, sm("stc", 8, 16)], writes=[R8])
            A("ACT", lambda e: e.activation(rtr8[:, :, :], rtr8[:, :, :],
                                            AF.Exp),
              reads=[R8], writes=[R8], tag="rexp")
            A("DVE", lambda e: e.reduce_sum(sm8, rtr8[:, :, :], axis=AXX),
              reads=[R8], writes=[sm("stc", 24, 32)])
            A("DVE", lambda e: e.reciprocal(sm8, sm8),
              reads=[sm("stc", 24, 32)], writes=[sm("stc", 24, 32)])
            A("DVE", lambda e: e.tensor_tensor(
                rtr8[:, :, :], rtr8[:, :, :], bc8(sm8), ALU.mult),
                reads=[R8, sm("stc", 24, 32)], writes=[R8])
            A("DVE", lambda e: e.tensor_copy(prw8[:, :, :], rtr8[:, :, :]),
              reads=[R8], writes=[P8])
            for j in range(4):
                A("DVE", lambda e: e.reduce_max(nm8, prw8[:, :, :],
                                                axis=AXX),
                  reads=[P8], writes=[sm("stc", 8, 16)])
                if j < 3:
                    A("DVE", lambda e: e.tensor_tensor(
                        cw8[:, :, :], prw8[:, :, :], bc8(nm8), ALU.is_lt),
                        reads=[P8, sm("stc", 8, 16)], writes=[C8])
                    A("DVE", lambda e: e.tensor_tensor(
                        prw8[:, :, :], prw8[:, :, :], cw8[:, :, :],
                        ALU.mult),
                        reads=[P8, C8], writes=[P8])
            A("DVE", lambda e: e.tensor_tensor(
                prw8[:, :, :], rtr8[:, :, :], bc8(nm8), ALU.is_ge),
                reads=[R8, sm("stc", 8, 16)], writes=[P8])
            A("DVE", lambda e: e.tensor_tensor(
                prw8[:, :, :], prw8[:, :, :], rtr8[:, :, :], ALU.mult),
                reads=[P8, R8], writes=[P8])
            A("DVE", lambda e: e.reduce_sum(sm8, prw8[:, :, :], axis=AXX),
              reads=[P8], writes=[sm("stc", 24, 32)])
            A("DVE", lambda e: e.tensor_scalar_add(sm8, sm8, 1e-8),
              reads=[sm("stc", 24, 32)], writes=[sm("stc", 24, 32)])
            A("DVE", lambda e: e.reciprocal(sm8, sm8),
              reads=[sm("stc", 24, 32)], writes=[sm("stc", 24, 32)])
            A("DVE", lambda e: e.tensor_tensor(
                prw8[:, :, :], prw8[:, :, :], bc8(sm8), ALU.mult),
                reads=[P8, sm("stc", 24, 32)], writes=[P8])
            A("DVE", lambda e: e.tensor_tensor(
                cw8[:, :, :], prw8[:, :, :],
                es_b[:, None, :].broadcast_to([128, HLF, E]), ALU.mult),
                reads=[P8, sm("esb")], writes=[C8], tag="cw8")
            def d1_tail(ti):
                tj = ti % 2
                tk2 = 2 + ti % 2
                A("PE", lambda e, ti_=ti, t_=tj: e.transpose(
                    tpp[t_][:, 0:E], cw8[:, ti_, :], idbf[:, :]),
                    reads=[C8, sm("idbf")], writes=[TPR[tj]])
                A("ACT", lambda e, t_=tj, ti_=ti: e.copy(
                    cwT[:, ti_ % 2, :], tpp[t_][:, 0:E]),
                  reads=[TPR[tj]], writes=[sm("cwT", ti % 2, ti % 2 + 1)])
                wb = (ti % 2) * 2
                A("PE", lambda e, bk=wb, ti_=ti: e.matmul(
                    mm[bk][:, 0:K], cwT[:, ti_ % 2, :], mix_bf[:, :],
                    start=True, stop=True),
                    reads=[sm("cwT", ti % 2, ti % 2 + 1), sm("mixbf")],
                    writes=[MMR[wb]], tag=f"wkmm{ti}")
                A("ACT", lambda e, ti_=ti, bk=wb: e.copy(
                    wk_g[:, ti_, :], mm[bk][:, 0:K]),
                    reads=[MMR[wb]], writes=[sm("wkg", ti, ti + 1)],
                    tag=f"wkev{ti}")
            for ch, (co, cwd) in enumerate(_chunks(KI, min(512, I))):
                wj = ch % 2
                kk = co // I
                WGIV = (RW, wj * NHB * 512, (wj + 1) * NHB * 512)
                WUIV = (RW, (2 + wj) * NHB * 512, (3 + wj) * NHB * 512)
                A("POOL", lambda e, jj=wj, co_=co, cw_=cwd: e.dma_start(
                    out=wg_st[jj][:, :, 0:cw_],
                    in_=Wg[:, co_:co_ + cw_].rearrange(
                        "(b p) c -> p b c", p=128)),
                    writes=[WGIV], dma=True, tag=f"ldwg{ch}")
                A("POOL", lambda e, jj=wj, co_=co, cw_=cwd: e.dma_start(
                    out=wu_st[jj][:, :, 0:cw_],
                    in_=Wu[:, co_:co_ + cw_].rearrange(
                        "(b p) c -> p b c", p=128)),
                    writes=[WUIV], dma=True, tag=f"ldwu{ch}")
                for ti in range(HLF):
                    par = ti % 2
                    bg = par * 2
                    bu = par * 2 + 1

                    def g_group(e, jj=wj, bk=bg, ti_=ti, cw_=cwd):
                        last = None
                        for hh in range(NHB):
                            last = e.matmul(
                                mm[bk][:, 0:cw_],
                                featA[:, hh, ti_ * 128:(ti_ + 1) * 128],
                                wg_st[jj][:, hh, 0:cw_],
                                start=(hh == 0), stop=(hh == NHB - 1))
                        return last
                    A("PE", g_group, reads=[WGIV, FEAT_G], writes=[MMR[bg]],
                      tag=f"gmm{ch}_{ti}")

                    def u_group(e, jj=wj, bk=bu, ti_=ti, cw_=cwd):
                        last = None
                        for hh in range(NHB):
                            last = e.matmul(
                                mm[bk][:, 0:cw_],
                                featA[:, hh, ti_ * 128:(ti_ + 1) * 128],
                                wu_st[jj][:, hh, 0:cw_],
                                start=(hh == 0), stop=(hh == NHB - 1))
                        return last
                    A("PE", u_group, reads=[WUIV, FEAT_G], writes=[MMR[bu]],
                      tag=f"umm{ch}_{ti}")
                    if ch == 0:
                        d1_tail(ti)
                    A("ACT", lambda e, bk=bg, pp=par, cw_=cwd: e.activation(
                        sgsD[pp][:, 0:cw_], mm[bk][:, 0:cw_], AF.Silu),
                        reads=[MMR[bg]], writes=[ru_iv(par, 0, 1024)],
                        tag=f"silu{ch}_{ti}")
                    A("DVE", lambda e, bk=bu, pp=par, ti_=ti, kk_=kk,
                      cw_=cwd: e.scalar_tensor_tensor(
                          tbD[pp][:, 0:cw_], sgsD[pp][:, 0:cw_],
                          wk_g[:, ti_, kk_:kk_ + 1], mm[bk][:, 0:cw_],
                          ALU.mult, ALU.mult),
                      reads=[ru_iv(par, 0, 1024), sm("wkg", ti, ti + 1),
                             MMR[bu]],
                      writes=[ru_iv(par, 1024, 2048)], tag=f"tb{ch}_{ti}")
                    ci0 = co % I
                    CIV = (RV, 2 * (ti * I + ci0), 2 * (ti * I + ci0 + cwd))
                    if kk == 0:
                        A("DVE", lambda e, pp=par, ti_=ti, ci=ci0, cw_=cwd:
                          e.tensor_copy(cacc[:, ti_, ci:ci + cw_],
                                        tbD[pp][:, 0:cw_]),
                          reads=[ru_iv(par, 1024, 2048)], writes=[CIV],
                          tag=f"cset{ch}_{ti}")
                    else:
                        A("DVE", lambda e, pp=par, ti_=ti, ci=ci0, cw_=cwd:
                          e.tensor_tensor(cacc[:, ti_, ci:ci + cw_],
                                          cacc[:, ti_, ci:ci + cw_],
                                          tbD[pp][:, 0:cw_], ALU.add),
                          reads=[ru_iv(par, 1024, 2048), CIV], writes=[CIV],
                          tag=f"cacc{ch}_{ti}")
            for ti in range(HLF):
                CAV = (RV, 2 * ti * I, 2 * (ti + 1) * I)
                A("DVE", lambda e, ti_=ti: e.tensor_copy(
                    cbf[:, :], cacc[:, ti_, :]),
                    reads=[CAV], writes=[(RV, CBO, CBO + I)], tag=f"cbf{ti}")
                for ib in range(NIB):
                    tj = ib % 4
                    A("PE", lambda e, ib_=ib, tj_=tj: e.transpose(
                        tpp[tj_][:, 0:128],
                        cbf[:, ib_ * 128:(ib_ + 1) * 128], idbf[:, :]),
                        reads=[(RV, CBO, CBO + I), sm("idbf")],
                        writes=[TPR[tj]], tag=f"ctp{ti}_{ib}")
                    A("DVE", lambda e, ib_=ib, tj_=tj, ti_=ti: e.tensor_copy(
                        cT[:, ib_, ti_ * 128:(ti_ + 1) * 128],
                        tpp[tj_][:, 0:128]),
                        reads=[TPR[tj]],
                        writes=[(RV, CTO + ib * GW + ti * 128,
                                 CTO + ib * GW + ti * 128 + 128)])
            CT_ALL = (RV, CTO, CTO + NIB * GW)
            for ch in range(NI2B):
                wj = ch % 2
                WSGIV = (RW, WSGO + wj * NHB * 128,
                         WSGO + (wj + 1) * NHB * 128)
                WSUIV = (RW, WSUO + wj * NHB * 128,
                         WSUO + (wj + 1) * NHB * 128)
                A("POOL", lambda e, jj=wj, ch_=ch: e.dma_start(
                    out=wsg_st[jj][:, :, :],
                    in_=Wsg[:, ch_ * 128:(ch_ + 1) * 128].rearrange(
                        "(b p) c -> p b c", p=128)),
                    writes=[WSGIV], dma=True, tag=f"ldwsg{ch}")
                A("POOL", lambda e, jj=wj, ch_=ch: e.dma_start(
                    out=wsu_st[jj][:, :, :],
                    in_=Wsu[:, ch_ * 128:(ch_ + 1) * 128].rearrange(
                        "(b p) c -> p b c", p=128)),
                    writes=[WSUIV], dma=True, tag=f"ldwsu{ch}")
                for ts_, (toff, tw) in enumerate(_chunks(GW, 512)):
                    par = ts_ % 2
                    bg = par * 2
                    bu = par * 2 + 1

                    def sg_group(e, jj=wj, bk=bg, to=toff, tw_=tw):
                        last = None
                        for hh in range(NHB):
                            last = e.matmul(
                                mm[bk][:, 0:tw_], wsg_st[jj][:, hh, :],
                                featA[:, hh, to:to + tw_],
                                start=(hh == 0), stop=(hh == NHB - 1))
                        return last
                    A("PE", sg_group, reads=[WSGIV, FEAT_G],
                      writes=[MMR[bg]], tag=f"sgmm{ch}_{ts_}")

                    def su_group(e, jj=wj, bk=bu, to=toff, tw_=tw):
                        last = None
                        for hh in range(NHB):
                            last = e.matmul(
                                mm[bk][:, 0:tw_], wsu_st[jj][:, hh, :],
                                featA[:, hh, to:to + tw_],
                                start=(hh == 0), stop=(hh == NHB - 1))
                        return last
                    A("PE", su_group, reads=[WSUIV, FEAT_G],
                      writes=[MMR[bu]], tag=f"summ{ch}_{ts_}")
                    A("ACT", lambda e, bk=bg, pp=par, tw_=tw: e.activation(
                        sgs2[pp][:, 0:tw_], mm[bk][:, 0:tw_], AF.Silu),
                        reads=[MMR[bg]],
                        writes=[(RV, SG2O + par * 1024,
                                 SG2O + (par + 1) * 1024)],
                        tag=f"ssil{ch}_{ts_}")
                    A("DVE", lambda e, bk=bu, pp=par, ch_=ch, to=toff,
                      tw_=tw: e.tensor_tensor(
                          hshT[:, ch_, to:to + tw_], sgs2[pp][:, 0:tw_],
                          mm[bk][:, 0:tw_], ALU.mult),
                      reads=[(RV, SG2O + par * 1024,
                              SG2O + (par + 1) * 1024), MMR[bu]],
                      writes=[(RW, ch * GW + toff, ch * GW + toff + tw)],
                      tag=f"hsh{ch}_{ts_}")
            HSH_ALL = (RW, 0, NI2B * GW)
            for hc, (ho, hw) in enumerate(_chunks(H, 512)):
                wj = hc % 2
                WDIV = (RV, wj * NIB * 512, (wj + 1) * NIB * 512)
                WSDIV = (RV, WSDO + wj * NI2B * 512,
                         WSDO + (wj + 1) * NI2B * 512)
                A("POOL", lambda e, jj=wj, ho_=ho, hw_=hw: e.dma_start(
                    out=wd_st[jj][:, :, 0:hw_],
                    in_=Wd[:, ho_:ho_ + hw_].rearrange(
                        "(b p) c -> p b c", p=128)),
                    writes=[WDIV], dma=True, tag=f"ldwd{hc}")
                A("POOL", lambda e, jj=wj, ho_=ho, hw_=hw: e.dma_start(
                    out=wsd_st[jj][:, :, 0:hw_],
                    in_=Wsd[:, ho_:ho_ + hw_].rearrange(
                        "(b p) c -> p b c", p=128)),
                    writes=[WSDIV], dma=True, tag=f"ldwsd{hc}")
                for ti in range(HLF):
                    t = gt0 + ti
                    par = ti % 2
                    bank = (hc * HLF + ti) % 4
                    A("ACT", lambda e, tt=t, pp=par, ho_=ho, hw_=hw:
                      e.dma_start(
                          out=x_sl[pp][0:128, 0:hw_],
                          in_=X[b, tt * 128:(tt + 1) * 128, ho_:ho_ + hw_]),
                      writes=[rx_iv(par, 0, 512)], dma=True)
                    A("ACT", lambda e, tt=t, pp=par, ho_=ho, hw_=hw:
                      e.dma_start(
                          out=s_sl[pp][0:128, 0:hw_],
                          in_=s_sp[b, tt * 128:(tt + 1) * 128,
                                   ho_:ho_ + hw_]),
                      reads=[(RD["d_s"], b * NT + t, b * NT + t + 1)],
                      writes=[ru_iv(par, 512, 1024)], dma=True)

                    def d_group(e, jj=wj, bk=bank, ti_=ti, hw_=hw):
                        last = None
                        for ib_ in range(NIB):
                            last = e.matmul(
                                mm[bk][:, 0:hw_],
                                cT[:, ib_, ti_ * 128:(ti_ + 1) * 128],
                                wd_st[jj][:, ib_, 0:hw_],
                                start=(ib_ == 0), stop=False)
                        for i2_ in range(NI2B):
                            last = e.matmul(
                                mm[bk][:, 0:hw_],
                                hshT[:, i2_, ti_ * 128:(ti_ + 1) * 128],
                                wsd_st[jj][:, i2_, 0:hw_],
                                start=False, stop=(i2_ == NI2B - 1))
                        return last
                    A("PE", d_group,
                      reads=[CT_ALL, WDIV, HSH_ALL, WSDIV],
                      writes=[MMR[bank]], tag=f"dmm{hc}_{ti}")
                    A("ACT", lambda e, bk=bank, pp=par, hw_=hw: e.activation(
                        gsb[pp][:, 0:hw_], mm[bk][:, 0:hw_], AF.Silu),
                        reads=[MMR[bank]], writes=[rx_iv(par, 512, 1024)],
                        tag=f"gsil{hc}_{ti}")
                    A("DVE", lambda e, pp=par, hw_=hw: e.tensor_tensor(
                        tb2[pp][:, 0:hw_], gsb[pp][:, 0:hw_],
                        s_sl[pp][0:128, 0:hw_], ALU.mult),
                        reads=[rx_iv(par, 512, 1024), ru_iv(par, 512, 1024)],
                        writes=[rx_iv(par, 1024, 1536)], tag=f"sg{hc}_{ti}")
                    A("DVE", lambda e, pp=par, hw_=hw: e.tensor_tensor(
                        ostage[pp][:, 0:hw_], tb2[pp][:, 0:hw_],
                        x_sl[pp][0:128, 0:hw_], ALU.add),
                        reads=[rx_iv(par, 1024, 1536), rx_iv(par, 0, 512)],
                        writes=[rx_iv(par, 1536, 2048)], tag=f"fin{hc}_{ti}")
                    A("ACT", lambda e, tt=t, pp=par, ho_=ho, hw_=hw:
                      e.dma_start(
                          out=OUT[b, tt * 128:(tt + 1) * 128, ho_:ho_ + hw_],
                          in_=ostage[pp][:, 0:hw_]),
                      reads=[rx_iv(par, 1536, 2048)],
                      writes=[dr("d_out", b)], dma=True,
                      tag=f"stout{hc}_{ti}")

    for _b in range(B_CORE):
        emit_batch(_b)

    with nc.Block() as block:
        s.finalize(block, st)
    return st


def kernel(**inputs):
    D = REAL_DIMS
    n_cores = 8
    x = np.ascontiguousarray(inputs["L0_attn_x"], dtype=np.float32)
    xs = np.split(x, n_cores, axis=0)
    nc = bass.Bass("TRN2", target_bir_lowering=False)
    build_program(nc, D)
    weights = {k: np.ascontiguousarray(v, dtype=np.float32)
               for k, v in inputs.items()
               if k not in ("L0_attn_x", "L0_attn_gamma", "L0_attn_beta")}
    weights["gamma_pb"] = np.ascontiguousarray(
        np.asarray(inputs["L0_attn_gamma"], dtype=np.float32)
        .reshape(-1, 128).T)
    weights["beta_pb"] = np.ascontiguousarray(
        np.asarray(inputs["L0_attn_beta"], dtype=np.float32)
        .reshape(-1, 128).T)
    in_maps = [dict(weights, L0_attn_x=xs[c]) for c in range(n_cores)]
    res = run_bass_kernel_spmd(nc, in_maps, core_ids=list(range(n_cores)))
    return np.concatenate([r["out"] for r in res.results], axis=0)


# version 11
# speedup vs baseline: 1.0353x; 1.0109x over previous
import numpy as np
from contextlib import ExitStack

import concourse.bass as bass
import concourse.mybir as mybir
from concourse.bass_utils import run_bass_kernel_spmd

F32 = mybir.dt.float32
BF16 = mybir.dt.bfloat16
AF = mybir.ActivationFunctionType
ALU = mybir.AluOpType
AXX = mybir.AxisListType.X

ENGINES = ("PE", "DVE", "ACT", "POOL", "SP")



class Region:
    __slots__ = ("name", "recs")

    def __init__(self, name):
        self.name = name
        self.recs = []


class _Op:
    __slots__ = ("engine", "emit", "deps", "is_dma", "seq", "tag",
                 "sem", "val", "need_inc", "waits")

    def __init__(self, engine, emit, is_dma, tag):
        self.engine = engine
        self.emit = emit
        self.deps = []
        self.is_dma = is_dma
        self.seq = -1
        self.tag = tag
        self.sem = None
        self.val = 0
        self.need_inc = False
        self.waits = []


class Sched:
    def __init__(self, nc, strict=True):
        self.nc = nc
        self.ops = []
        self.regions = {}
        self.strict = strict

    def region(self, name):
        r = self.regions.get(name)
        if r is None:
            r = self.regions[name] = Region(name)
        return r

    def add(self, engine, emit, reads=(), writes=(), dma=False, tag=""):
        op = _Op(engine, emit, dma, tag)
        op.seq = len(self.ops)
        dep_set = {}
        for (reg, a, b) in reads:
            for rec in reg.recs:
                if rec[0] < b and a < rec[1]:
                    w = rec[2]
                    if w is not None:
                        dep_set[w.seq] = w
                    rec[3].append(op)
        for (reg, a, b) in writes:
            keep = []
            for rec in reg.recs:
                if rec[0] < b and a < rec[1]:
                    if rec[2] is not None:
                        dep_set[rec[2].seq] = rec[2]
                    for r_ in rec[3]:
                        dep_set[r_.seq] = r_
                    if rec[0] < a:
                        keep.append([rec[0], a, rec[2], list(rec[3])])
                    if b < rec[1]:
                        keep.append([b, rec[1], rec[2], list(rec[3])])
                else:
                    keep.append(rec)
            keep.append([a, b, op, []])
            reg.recs = keep
        op.deps = [d for d in dep_set.values() if d.seq != op.seq]
        self.ops.append(op)
        return op

    def finalize(self, block, stack):
        nc = self.nc
        for op in self.ops:
            if op.is_dma:
                op.need_inc = True
        for op in self.ops:
            for d in op.deps:
                if d.engine == "PE" and op.engine == "PE" and not d.is_dma \
                        and not op.is_dma:
                    continue
                if self.strict or d.is_dma or d.engine != op.engine:
                    d.need_inc = True
        SEM_MAX = 55000
        NSLOT = 8
        counters = {}
        sems = {}
        dma_idx = {}
        throttle = {}
        for op in self.ops:
            if op.is_dma:
                i = dma_idx.get(op.engine, 0)
                dma_idx[op.engine] = i + 1
                slot = i % NSLOT
                val = 16 * (i // NSLOT + 1)
                skey = (op.engine, "q", slot)
                if skey not in sems:
                    sems[skey] = stack.enter_context(
                        nc.semaphore(f"s_{op.engine}_q{slot}"))
                op.sem = sems[skey]
                op.val = val
                if val > 16:
                    throttle[op] = (op.sem, val - 16)
                continue
            key = (op.engine, "c")
            cnt, epoch = counters.get(key, (0, 0))
            if op.need_inc and cnt + 1 > SEM_MAX:
                epoch += 1
                cnt = 0
            if op.need_inc:
                cnt += 1
            counters[key] = (cnt, epoch)
            skey = key + (epoch,)
            if op.need_inc and skey not in sems:
                sems[skey] = stack.enter_context(
                    nc.semaphore(f"s_{key[0]}_c{epoch}"))
            op.sem = sems.get(skey)
            op.val = cnt
        water = {e: {} for e in ENGINES}
        for op in self.ops:
            w = water[op.engine]
            th = throttle.get(op)
            if th is not None:
                sem, val = th
                if w.get(id(sem), -1) < val:
                    op.waits.append((sem, val))
                    w[id(sem)] = val
            for d in op.deps:
                if (d.engine == "PE" and op.engine == "PE" and not d.is_dma
                        and not op.is_dma):
                    continue
                if not d.need_inc:
                    assert d.engine == op.engine and not d.is_dma
                    continue
                if (not self.strict and d.engine == op.engine
                        and not d.is_dma and not op.is_dma):
                    continue
                sid = id(d.sem)
                if w.get(sid, -1) < d.val:
                    op.waits.append((d.sem, d.val))
                    w[sid] = d.val
        per_engine = {e: [] for e in ENGINES}
        for op in self.ops:
            per_engine[op.engine].append(op)
        for e in ENGINES:
            q = [op for op in per_engine[e] if op.is_dma]
            if q:
                finals = {}
                for d in q:
                    cur = finals.get(id(d.sem))
                    if cur is None or d.val > cur[1]:
                        finals[id(d.sem)] = (d.sem, d.val)

                def mk_drain(pairs):
                    def drain(eng):
                        for sem, val in pairs:
                            eng.wait_ge(sem, val)
                        return None
                    return drain
                op = _Op(e, mk_drain(list(finals.values())), False, "drain")
                op.seq = len(self.ops)
                self.ops.append(op)
                per_engine[e].append(op)

        def mk_body(ename):
            ops = per_engine[ename]

            def body(eng):
                for op in ops:
                    for (sem, val) in op.waits:
                        eng.wait_ge(sem, val)
                    inst = op.emit(eng)
                    if op.need_inc:
                        if inst is None:
                            raise RuntimeError(f"op {op.tag} emitted None")
                        inst.then_inc(op.sem, 16 if op.is_dma else 1)
            return body

        if per_engine["PE"]:
            block.tensor(mk_body("PE"))
        if per_engine["DVE"]:
            block.vector(mk_body("DVE"))
        if per_engine["ACT"]:
            block.scalar(mk_body("ACT"))
        if per_engine["POOL"]:
            block.gpsimd(mk_body("POOL"))
        if per_engine["SP"]:
            block.sync(mk_body("SP"))



REAL_DIMS = dict(B_CORE=2, S=2048, H=2048, KEY=2048, KI=8192, I=1024, K=8,
                 ISH=2048, E=128)


def _chunks(total, step):
    out = []
    o = 0
    while o < total:
        w = min(step, total - o)
        out.append((o, w))
        o += w
    return out


def build_program(nc, D):
    B_CORE, S, H = D["B_CORE"], D["S"], D["H"]
    KEY, KI, I, K, ISH, E = (D["KEY"], D["KI"], D["I"], D["K"], D["ISH"],
                             D["E"])
    NT = S // 128
    HLF = NT // 2
    NHB = H // 128
    NKB = KEY // 128
    NIB = I // 128
    NI2B = ISH // 128
    SH = S // 2
    GW = HLF * 128
    nbn = max(1, H // 512)
    rscale = float(1.0 / np.sqrt(np.float32(KEY)))

    X = nc.dram_tensor("L0_attn_x", [B_CORE, S, H], F32,
                       kind="ExternalInput").ap()
    Wqkv = nc.dram_tensor("L0_attn_qkv", [H, 3 * KEY], F32,
                          kind="ExternalInput").ap()
    Wo = nc.dram_tensor("L0_attn_o_proj", [KEY, H], F32,
                        kind="ExternalInput").ap()
    Gam = nc.dram_tensor("gamma_pb", [128, H // 128], F32,
                         kind="ExternalInput").ap()
    Bet = nc.dram_tensor("beta_pb", [128, H // 128], F32,
                         kind="ExternalInput").ap()
    Wr = nc.dram_tensor("L0_ffn_w_router", [H, E], F32,
                        kind="ExternalInput").ap()
    Wg = nc.dram_tensor("L0_ffn_w_gate", [H, KI], F32,
                        kind="ExternalInput").ap()
    Wu = nc.dram_tensor("L0_ffn_w_up", [H, KI], F32,
                        kind="ExternalInput").ap()
    Wd = nc.dram_tensor("L0_ffn_w_down", [I, H], F32,
                        kind="ExternalInput").ap()
    Amp = nc.dram_tensor("L0_ffn_amplitudes", [E, K, 2], F32,
                         kind="ExternalInput").ap()
    Esc = nc.dram_tensor("L0_ffn_expert_scale", [E], F32,
                         kind="ExternalInput").ap()
    Wsg = nc.dram_tensor("L0_ffn_shared_w_gate", [H, ISH], F32,
                         kind="ExternalInput").ap()
    Wsu = nc.dram_tensor("L0_ffn_shared_w_up", [H, ISH], F32,
                         kind="ExternalInput").ap()
    Wsd = nc.dram_tensor("L0_ffn_shared_w_down", [ISH, H], F32,
                         kind="ExternalInput").ap()
    OUT = nc.dram_tensor("out", [B_CORE, S, H], F32,
                         kind="ExternalOutput").ap()
    q_sp = nc.dram_tensor("q_sp", [B_CORE, NKB, 128, S], BF16,
                          kind="Internal").ap()
    kT_sp = nc.dram_tensor("kT_sp", [B_CORE, NKB, 128, S], BF16,
                           kind="Internal").ap()
    xnT_sp = nc.dram_tensor("xnT_sp", [B_CORE, NHB, 128, S], BF16,
                            kind="Internal").ap()
    u_sp = nc.dram_tensor("u_sp", [B_CORE, S, H], BF16,
                          kind="Internal").ap()
    avT_sp = nc.dram_tensor("avT_sp", [B_CORE, NKB, 128, S], BF16,
                            kind="Internal").ap()
    s_sp = nc.dram_tensor("s_sp", [B_CORE, S, H], BF16,
                          kind="Internal").ap()

    st = ExitStack()
    sbt = lambda n, shp, dt: st.enter_context(nc.sbuf_tensor(n, shp, dt))
    pst = lambda n, shp, dt: st.enter_context(nc.psum_tensor(n, shp, dt))

    VEL = max(NT * KEY,
              2 * HLF * I + I + 2 * 1024 + NIB * GW,
              2 * NIB * 512 + 2 * NI2B * 512 + NIB * GW)
    WEL = max(3 * NHB * 128 + 2 * NHB * 512 + 2 * 512,
              2 * NKB * 512 + 2 * NKB * 128,
              NKB * H,
              4 * NHB * 512,
              NI2B * GW + 4 * NHB * 128)
    FEL = max(NHB * SH,
              2 * S + 2 * S + S + 3 * NKB * 128,
              2 * NKB * 128)
    r_v = sbt("r_v", [128, VEL], BF16)
    r_w = sbt("r_w", [128, WEL], BF16)
    r_feat = sbt("r_feat", [128, FEL], BF16)
    XUW = max(H, 2048)
    r_x = sbt("r_x", [128, 2, XUW], F32)
    r_u = sbt("r_u", [128, 2, XUW], BF16)
    r_stg = sbt("r_stg", [128, 2, 512 + NHB * 128], BF16)

    idbf = sbt("idbf", [128, 128], BF16)
    idf32 = sbt("idf32", [128, 128], F32)
    causal = sbt("causal", [128, 128], F32)
    onesf = sbt("onesf", [1, 128], F32)
    stc = sbt("stc", [128, 2, 16], F32)
    zbuf = sbt("zbuf", [128, max(H, 512)], BF16)
    bnst = sbt("bnst", [128, 2, nbn, 6], F32)
    ag12 = sbt("ag12", [128, 2, 4], F32)
    cc_ = sbt("cc_", [128, 5], F32)
    gam_sb = sbt("gam_sb", [128, NHB], F32)
    bet_sb = sbt("bet_sb", [128, NHB], F32)
    wr_sb = sbt("wr_sb", [128, NHB, E], BF16)
    amp_sb = sbt("amp_sb", [128, 2 * K], F32)
    mixp = sbt("mixp", [128, K], F32)
    mix_bf = sbt("mix_bf", [128, K], BF16)
    es_row = sbt("es_row", [1, E], F32)
    es_b = sbt("es_b", [128, E], F32)
    rtr = sbt("rtr", [128, E], F32)
    cwT = sbt("cwT", [128, 2, E], BF16)
    wk_g = sbt("wk_g", [128, HLF, K], F32)
    nmx = sbt("nmx", [128, 1], F32)
    ssm = sbt("ssm", [128, 1], F32)
    rsb = sbt("rsb", [128, 1], F32)

    mm = [pst(f"mm{i}", [128, 512], F32) for i in range(4)]
    tpp = [pst(f"tp{i}", [128, 1024], BF16) for i in range(4)]

    s = Sched(nc)

    def reg(n):
        return s.region(n)

    RV, RW, RF = reg("v"), reg("w"), reg("feat")
    RX, RU, RS = reg("x"), reg("u"), reg("stg")
    SMALLS = ("idbf idf32 causal onesf statT bcb stc bnst ag12 cc gam bet "
              "wr amp mixp mixt mixbf esrow esb rtr prb prw eqm mskp cwbf "
              "cwT wkg zbuf nmx ssm rsb").split()
    RSM = {n: reg(n) for n in SMALLS}
    PS = {n: reg(n) for n in
          ("mm0", "mm1", "mm2", "mm3", "tp0", "tp1", "tp2", "tp3")}
    RD = {n: reg(n) for n in
          ("d_q", "d_k", "d_xnT", "d_u", "d_avT", "d_s", "d_out")}
    MMR = [(PS[f"mm{i}"], 0, 1) for i in range(4)]
    TPR = [(PS[f"tp{i}"], 0, 1) for i in range(4)]

    def sm(n, a=0, b=1 << 30):
        return (RSM[n], a, b)

    def dr(n, b_):
        return (RD[n], b_, b_ + 1)

    A = s.add

    vres = r_v[:, 0:NT * KEY].rearrange("p (t d) -> p t d", d=KEY)
    featA = r_feat[:, 0:NHB * SH].rearrange("p (h t) -> p h t", t=SH)
    wqkv_st = [r_w[:, j * NHB * 128:(j + 1) * NHB * 128]
               .rearrange("p (h c) -> p h c", c=128) for j in range(3)]
    VSTO = 3 * NHB * 128
    v_st = [r_w[:, VSTO + j * NHB * 512:VSTO + (j + 1) * NHB * 512]
            .rearrange("p (h c) -> p h c", c=512) for j in range(2)]
    KSTO = VSTO + 2 * NHB * 512
    kstage = [r_w[:, KSTO + j * 512:KSTO + (j + 1) * 512] for j in range(2)]
    k_b = [r_w[:, j * NKB * 512:(j + 1) * NKB * 512]
           .rearrange("p (d c) -> p d c", c=512) for j in range(2)]
    AVSO = 2 * NKB * 512
    wo_full = r_w[:, 0:NKB * H].rearrange("p (d c) -> p d c", c=H)
    scores = r_feat[:, 0:2 * S].bitcast(F32)
    pbuf = r_feat[:, 2 * S:4 * S].bitcast(F32)
    probs_bf = r_feat[:, 4 * S:5 * S]
    probsT = r_feat[:, 5 * S:5 * S + NKB * 128].rearrange(
        "p (d t) -> p d t", t=128)
    QTO = 5 * S + NKB * 128
    qtile = [r_feat[:, QTO + j * NKB * 128:QTO + (j + 1) * NKB * 128]
             .rearrange("p (d t) -> p d t", t=128) for j in range(2)]
    stg_flat = r_stg[:, :, :].rearrange("p a b -> p (a b)")
    avstage = [stg_flat[:, j * NKB * 128:(j + 1) * NKB * 128]
               .rearrange("p (d c) -> p d c", c=128) for j in range(2)]
    av_b = [stg_flat[:, j * NKB * 128:(j + 1) * NKB * 128]
            .rearrange("p (d t) -> p d t", t=128) for j in range(2)]
    rtr8 = stg_flat[:, 0:2 * HLF * E].bitcast(F32).rearrange(
        "p (t e) -> p t e", e=E)
    prw8 = stg_flat[:, 2 * HLF * E:4 * HLF * E].bitcast(F32).rearrange(
        "p (t e) -> p t e", e=E)
    cw8 = stg_flat[:, 4 * HLF * E:5 * HLF * E].rearrange(
        "p (t e) -> p t e", e=E)
    wg_st = [r_w[:, j * NHB * 512:(j + 1) * NHB * 512]
             .rearrange("p (h c) -> p h c", c=512) for j in range(2)]
    wu_st = [r_w[:, (2 + j) * NHB * 512:(3 + j) * NHB * 512]
             .rearrange("p (h c) -> p h c", c=512) for j in range(2)]
    hshT = r_w[:, 0:NI2B * GW].rearrange("p (d t) -> p d t", t=GW)
    WSGO = NI2B * GW
    wsg_st = [r_w[:, WSGO + j * NHB * 128:WSGO + (j + 1) * NHB * 128]
              .rearrange("p (h c) -> p h c", c=128) for j in range(2)]
    WSUO = WSGO + 2 * NHB * 128
    wsu_st = [r_w[:, WSUO + j * NHB * 128:WSUO + (j + 1) * NHB * 128]
              .rearrange("p (h c) -> p h c", c=128) for j in range(2)]
    cacc = r_v[:, 0:2 * HLF * I].bitcast(F32).rearrange(
        "p (t i) -> p t i", i=I)
    CBO = 2 * HLF * I
    cbf = r_v[:, CBO:CBO + I]
    SG2O = CBO + I
    sgs2 = [r_v[:, SG2O + j * 1024:SG2O + (j + 1) * 1024].bitcast(F32)
            for j in range(2)]
    CTO = VEL - NIB * GW
    cT = r_v[:, CTO:CTO + NIB * GW].rearrange("p (d t) -> p d t", t=GW)
    wd_st = [r_v[:, j * NIB * 512:(j + 1) * NIB * 512]
             .rearrange("p (d c) -> p d c", c=512) for j in range(2)]
    WSDO = 2 * NIB * 512
    wsd_st = [r_v[:, WSDO + j * NI2B * 512:WSDO + (j + 1) * NI2B * 512]
              .rearrange("p (d c) -> p d c", c=512) for j in range(2)]
    assert WSDO + 2 * NI2B * 512 <= CTO, "wd/wsd overlap cT"
    u_sl = [r_u[:, j, 0:512] for j in range(2)]
    s_sl = [r_u[:, j, 512:1024] for j in range(2)]
    sstage = [r_u[:, j, 1024:1536] for j in range(2)]
    sgsD = [r_u[:, j, 0:1024].bitcast(F32) for j in range(2)]
    tbD = [r_u[:, j, 1024:2048].bitcast(F32) for j in range(2)]
    x_sl = [r_x[:, j, 0:512] for j in range(2)]
    gsb = [r_x[:, j, 512:1024] for j in range(2)]
    tb2 = [r_x[:, j, 1024:1536] for j in range(2)]
    ostage = [r_x[:, j, 1536:2048] for j in range(2)]
    qstage = [r_stg[:, j, 0:512] for j in range(2)]
    xnstage = [r_stg[:, j, 512:512 + NHB * 128].rearrange(
        "p (h c) -> p h c", c=128) for j in range(2)]

    def ru_iv(par, a, b):
        return (RU, par * XUW + a, par * XUW + b)

    def rx_iv(par, a, b):
        return (RX, par * XUW + a, par * XUW + b)

    def rstg_iv(par, a, b):
        W2 = 512 + NHB * 128
        return (RS, par * W2 + a, par * W2 + b)

    A("POOL", lambda e: e.memset(cc_[:, 0:1], 1e-5), writes=[sm("cc", 0, 1)])
    A("POOL", lambda e: e.memset(cc_[:, 1:2], -0.5), writes=[sm("cc", 1, 2)])
    A("POOL", lambda e: e.memset(cc_[:, 2:3], 1e-10), writes=[sm("cc", 2, 3)])
    A("POOL", lambda e: e.memset(cc_[:, 3:4], 1.00001),
      writes=[sm("cc", 3, 4)])
    A("POOL", lambda e: e.memset(cc_[:, 4:5], 0.5), writes=[sm("cc", 4, 5)])
    A("POOL", lambda e: e.memset(idbf[:, :], 0.0), writes=[sm("idbf")])
    A("POOL", lambda e: e.affine_select(
        out=idbf[:, :], in_=idbf[:, :], compare_op=ALU.not_equal, fill=1.0,
        base=0, pattern=[[-1, 128]], channel_multiplier=1),
        reads=[sm("idbf")], writes=[sm("idbf")])
    A("POOL", lambda e: e.memset(idf32[:, :], 0.0), writes=[sm("idf32")])
    A("POOL", lambda e: e.affine_select(
        out=idf32[:, :], in_=idf32[:, :], compare_op=ALU.not_equal, fill=1.0,
        base=0, pattern=[[-1, 128]], channel_multiplier=1),
        reads=[sm("idf32")], writes=[sm("idf32")])
    A("POOL", lambda e: e.memset(causal[:, :], 0.0), writes=[sm("causal")])
    A("POOL", lambda e: e.affine_select(
        out=causal[:, :], in_=causal[:, :], compare_op=ALU.is_ge, fill=-1e30,
        base=0, pattern=[[-1, 128]], channel_multiplier=1),
        reads=[sm("causal")], writes=[sm("causal")])
    A("POOL", lambda e: e.memset(onesf[:, :], 1.0), writes=[sm("onesf")])
    A("POOL", lambda e: e.dma_start(
        out=wr_sb[:, :, :], in_=Wr.rearrange("(b p) c -> p b c", p=128)),
        writes=[sm("wr")], dma=True)
    A("SP", lambda e: e.dma_start(out=gam_sb[:, :], in_=Gam[:, :]),
      writes=[sm("gam")], dma=True)
    A("SP", lambda e: e.dma_start(out=bet_sb[:, :], in_=Bet[:, :]),
      writes=[sm("bet")], dma=True)
    A("SP", lambda e: e.dma_start(
        out=amp_sb[:, :], in_=Amp.rearrange("e k t -> e (k t)")),
        writes=[sm("amp")], dma=True)
    A("SP", lambda e: e.dma_start(out=es_row[0:1, :], in_=Esc[None, :]),
      writes=[sm("esrow")], dma=True)
    A("DVE", lambda e: e.tensor_mul(mixp[:, :], amp_sb[:, 0:2 * K:2],
                                    amp_sb[:, 0:2 * K:2]),
      reads=[sm("amp")], writes=[sm("mixp")])
    A("DVE", lambda e: e.tensor_mul(rtr[:, 0:K], amp_sb[:, 1:2 * K:2],
                                    amp_sb[:, 1:2 * K:2]),
      reads=[sm("amp")], writes=[sm("rtr")])
    A("DVE", lambda e: e.tensor_add(mixp[:, :], mixp[:, :], rtr[:, 0:K]),
      reads=[sm("mixp"), sm("rtr")], writes=[sm("mixp")])
    A("DVE", lambda e: e.reduce_sum(stc[:, 0, 0:1], mixp[:, :], axis=AXX),
      reads=[sm("mixp")], writes=[sm("stc", 0, 1)])
    A("DVE", lambda e: e.tensor_scalar_add(stc[:, 0, 0:1], stc[:, 0, 0:1], 1e-8),
      reads=[sm("stc", 0, 1)], writes=[sm("stc", 0, 1)])
    A("DVE", lambda e: e.reciprocal(stc[:, 0, 1:2], stc[:, 0, 0:1]),
      reads=[sm("stc", 0, 1)], writes=[sm("stc", 1, 2)])
    A("DVE", lambda e: e.tensor_scalar(mix_bf[:, :], mixp[:, :], stc[:, 0, 1:2],
                                       None, ALU.mult),
      reads=[sm("mixp"), sm("stc", 1, 2)], writes=[sm("mixbf")])
    A("PE", lambda e: e.matmul(mm[0][0:128, 0:E], onesf[0:1, 0:128],
                               es_row[0:1, :], start=True, stop=True),
      reads=[sm("onesf"), sm("esrow")], writes=[MMR[0]])
    A("DVE", lambda e: e.tensor_copy(es_b[:, :], mm[0][0:128, 0:E]),
      reads=[MMR[0]], writes=[sm("esb")])

    def emit_batch(b):
        zb_v = [r_x[:, j, 0:(H + 1) // 2].bitcast(BF16) for j in range(2)]

        def a1_head(t):
            par = t % 2
            A("SP", lambda e, tt=t, pp=par: e.dma_start(
                out=r_x[:, pp, 0:H],
                in_=X[b, tt * 128:(tt + 1) * 128, :]),
                writes=[rx_iv(par, 0, H)], dma=True, tag=f"ldx{t}")
            for c, (off, wd_) in enumerate(_chunks(H, 512)):
                A("DVE", lambda e, pp=par, cc=c, oo=off, ww=wd_:
                  e.bn_stats(bnst[:, pp, cc, :], r_x[:, pp, oo:oo + ww]),
                  reads=[rx_iv(par, 0, H)],
                  writes=[sm("bnst", par * nbn + c, par * nbn + c + 1)])
            A("DVE", lambda e, pp=par: e.bn_aggr(ag12[:, pp, 0:2],
                                                 bnst[:, pp, :, :]),
              reads=[sm("bnst", par * nbn, par * nbn + nbn)],
              writes=[sm("ag12", par * 4, par * 4 + 2)])
            A("ACT", lambda e, pp=par: e.activation(
                stc[:, pp, 2:3], ag12[:, pp, 1:2], AF.Ln,
                bias=cc_[:, 0:1]),
                reads=[sm("ag12", par * 4, par * 4 + 2), sm("cc", 0, 1)],
                writes=[sm("stc", par * 16 + 2, par * 16 + 3)])
            A("ACT", lambda e, pp=par: e.activation(
                stc[:, pp, 3:4], stc[:, pp, 2:3], AF.Exp,
                scale=cc_[:, 1:2]),
                reads=[sm("stc", par * 16 + 2, par * 16 + 3),
                       sm("cc", 1, 2)],
                writes=[sm("stc", par * 16 + 3, par * 16 + 4)])
            A("DVE", lambda e, pp=par: e.tensor_scalar(
                r_u[:, pp, 0:H], r_x[:, pp, 0:H], ag12[:, pp, 0:1],
                stc[:, pp, 3:4], ALU.subtract, ALU.mult),
                reads=[rx_iv(par, 0, H), sm("ag12", par * 4, par * 4 + 2),
                       sm("stc", par * 16 + 3, par * 16 + 4)],
                writes=[ru_iv(par, 0, H)], tag=f"xn{t}")
            A("SP", lambda e, tt=t, pp=par: e.dma_start(
                out=u_sp[b, tt * 128:(tt + 1) * 128, :],
                in_=r_u[:, pp, 0:H]),
                reads=[ru_iv(par, 0, H)],
                writes=[(RD["d_u"], b * NT + t, b * NT + t + 1)],
                dma=True, tag=f"spu{t}")
            A("ACT", lambda e, pp=par: e.activation(
                stc[:, pp, 6:7], ag12[:, pp, 1:2], AF.Ln,
                bias=cc_[:, 2:3], scale=cc_[:, 3:4]),
                reads=[sm("ag12", par * 4, par * 4 + 2), sm("cc", 2, 4)],
                writes=[sm("stc", par * 16 + 6, par * 16 + 7)])
            A("DVE", lambda e, pp=par: e.tensor_tensor(
                stc[:, pp, 9:10], stc[:, pp, 2:3], stc[:, pp, 6:7],
                ALU.subtract),
                reads=[sm("stc", par * 16 + 2, par * 16 + 3),
                       sm("stc", par * 16 + 6, par * 16 + 7)],
                writes=[sm("stc", par * 16 + 9, par * 16 + 10)])
            A("ACT", lambda e, pp=par: e.activation(
                stc[:, pp, 7:8], stc[:, pp, 9:10], AF.Exp,
                scale=cc_[:, 4:5]),
                reads=[sm("stc", par * 16 + 9, par * 16 + 10),
                       sm("cc", 4, 5)],
                writes=[sm("stc", par * 16 + 7, par * 16 + 8)])
            A("DVE", lambda e, pp=par: e.tensor_scalar(
                zb_v[pp][:, 0:H], r_u[:, pp, 0:H], stc[:, pp, 7:8],
                None, ALU.mult),
                reads=[ru_iv(par, 0, H),
                       sm("stc", par * 16 + 7, par * 16 + 8)],
                writes=[rx_iv(par, 0, (H + 1) // 2)], tag=f"z{t}")

        def a1_tail(t):
            par = t % 2
            ti = t % HLF
            for hb in range(NHB):
                txn = hb % 2
                tz = 2 + hb % 2
                A("PE", lambda e, pp=par, hh=hb, tj=txn: e.transpose(
                    tpp[tj][:, 0:128],
                    r_u[:, pp, hh * 128:(hh + 1) * 128], idbf[:, :]),
                    reads=[ru_iv(par, 0, H), sm("idbf")],
                    writes=[TPR[txn]], tag=f"utp{t}_{hb}")
                A("ACT", lambda e, pp=par, hh=hb, tj=txn:
                  e.copy(xnstage[pp][:, hh, :], tpp[tj][:, 0:128]),
                  reads=[TPR[txn]],
                  writes=[rstg_iv(par, 512 + hb * 128,
                                  512 + hb * 128 + 128)])
                A("PE", lambda e, pp=par, hh=hb, tj=tz: e.transpose(
                    tpp[tj][:, 0:128],
                    zb_v[pp][:, hh * 128:(hh + 1) * 128], idbf[:, :]),
                    reads=[rx_iv(par, 0, (H + 1) // 2), sm("idbf")],
                    writes=[TPR[tz]], tag=f"ztp{t}_{hb}")
                A("DVE", lambda e, hh=hb, ti_=ti, tj=tz: e.tensor_scalar(
                    featA[:, hh, ti_ * 128:(ti_ + 1) * 128],
                    tpp[tj][:, 0:128], gam_sb[:, hh:hh + 1],
                    bet_sb[:, hh:hh + 1], ALU.mult, ALU.add),
                    reads=[TPR[tz], sm("gam"), sm("bet")],
                    writes=[(RF, hb * SH + ti * 128,
                             hb * SH + ti * 128 + 128)],
                    tag=f"xnorm{t}_{hb}")
            A("SP", lambda e, tt=t, pp=par: e.dma_start(
                out=xnT_sp[b, :, :, tt * 128:(tt + 1) * 128].rearrange(
                    "h p t -> p h t"),
                in_=xnstage[pp][:, :, :]),
                reads=[rstg_iv(par, 512, 512 + NHB * 128)],
                writes=[(RD["d_xnT"], b * 2 + t // HLF,
                         b * 2 + t // HLF + 1)],
                dma=True, tag=f"spxnT{t}")

        for half in range(2):
            t0 = half * HLF
            a1_head(t0)
            for ti in range(HLF):
                if ti + 1 < HLF:
                    a1_head(t0 + ti + 1)
                a1_tail(t0 + ti)
            FEAT_ALL = (RF, 0, NHB * SH)
            rot = 0
            for cc in range(2 * NKB):
                wj = cc % 3
                c0 = cc * 128
                WIV = (RW, wj * NHB * 128, (wj + 1) * NHB * 128)
                A("POOL", lambda e, jj=wj, c_=c0: e.dma_start(
                    out=wqkv_st[jj][:, :, :],
                    in_=Wqkv[:, c_:c_ + 128].rearrange(
                        "(b p) c -> p b c", p=128)),
                    writes=[WIV], dma=True, tag=f"ldqkv{cc}")
                for ts_, (toff, tw) in enumerate(_chunks(SH, 512)):
                    bank = rot % 4
                    rot += 1

                    def qk_group(e, jj=wj, bk=bank, to=toff, tw_=tw):
                        last = None
                        for hh in range(NHB):
                            last = e.matmul(
                                mm[bk][:, 0:tw_], wqkv_st[jj][:, hh, :],
                                featA[:, hh, to:to + tw_],
                                start=(hh == 0), stop=(hh == NHB - 1))
                        return last
                    A("PE", qk_group, reads=[WIV, FEAT_ALL],
                      writes=[MMR[bank]], tag=f"qkmm{cc}_{ts_}")
                    gtoff = half * SH + toff
                    qp = rot % 2
                    if cc < NKB:
                        A("DVE", lambda e, bk=bank, qq=qp, tw_=tw:
                          e.tensor_copy(qstage[qq][:, 0:tw_],
                                        mm[bk][:, 0:tw_]),
                          reads=[MMR[bank]], writes=[rstg_iv(qp, 0, 512)])
                        A("SP", lambda e, c_=cc, qq=qp, go=gtoff, tw_=tw:
                          e.dma_start(out=q_sp[b, c_, :, go:go + tw_],
                                      in_=qstage[qq][:, 0:tw_]),
                          reads=[rstg_iv(qp, 0, 512)],
                          writes=[(RD["d_q"], b * 2 + half,
                                   b * 2 + half + 1)], dma=True)
                    else:
                        dk = cc - NKB
                        A("DVE", lambda e, bk=bank, qq=qp, tw_=tw:
                          e.tensor_copy(kstage[qq][:, 0:tw_],
                                        mm[bk][:, 0:tw_]),
                          reads=[MMR[bank]],
                          writes=[(RW, KSTO + qp * 512,
                                   KSTO + qp * 512 + 512)])
                        A("SP", lambda e, d_=dk, qq=qp, go=gtoff, tw_=tw:
                          e.dma_start(out=kT_sp[b, d_, :, go:go + tw_],
                                      in_=kstage[qq][:, 0:tw_]),
                          reads=[(RW, KSTO + qp * 512, KSTO + qp * 512 + 512)],
                          writes=[(RD["d_k"], b * 2 + half,
                                   b * 2 + half + 1)], dma=True)
            for vc, (vo, vw) in enumerate(_chunks(KEY, 512)):
                vj = vc % 2
                VIV = (RW, VSTO + vj * NHB * 512, VSTO + (vj + 1) * NHB * 512)
                A("POOL", lambda e, jj=vj, vo_=vo, vw_=vw: e.dma_start(
                    out=v_st[jj][:, :, 0:vw_],
                    in_=Wqkv[:, 2 * KEY + vo_:2 * KEY + vo_ + vw_].rearrange(
                        "(b p) c -> p b c", p=128)),
                    writes=[VIV], dma=True, tag=f"ldv{vc}")
                for ti in range(HLF):
                    t = t0 + ti
                    bank = rot % 4
                    rot += 1

                    def v_group(e, jj=vj, bk=bank, ti_=ti, vw_=vw):
                        last = None
                        for hh in range(NHB):
                            last = e.matmul(
                                mm[bk][:, 0:vw_],
                                featA[:, hh, ti_ * 128:(ti_ + 1) * 128],
                                v_st[jj][:, hh, 0:vw_],
                                start=(hh == 0), stop=(hh == NHB - 1))
                        return last
                    A("PE", v_group, reads=[VIV, FEAT_ALL],
                      writes=[MMR[bank]], tag=f"vmm{vc}_{t}")
                    A("DVE", lambda e, bk=bank, tt=t, vo_=vo, vw_=vw:
                      e.tensor_copy(vres[:, tt, vo_:vo_ + vw_],
                                    mm[bk][:, 0:vw_]),
                      reads=[MMR[bank]],
                      writes=[(RV, t * KEY + vo, t * KEY + vo + vw)])

        V_ALL = (RV, 0, NT * KEY)
        scoresP = [scores, pbuf]

        def b_scores_pair(p2):
            iA, iB = 2 * p2, 2 * p2 + 1
            LB = (iB + 1) * 128
            for i_, qp in ((iA, 0), (iB, 1)):
                QIV = (RF, QTO + qp * NKB * 128, QTO + (qp + 1) * NKB * 128)
                A("SP", lambda e, ii=i_, qq=qp: e.dma_start(
                    out=qtile[qq][:, :, :],
                    in_=q_sp[b, :, :, ii * 128:(ii + 1) * 128].rearrange(
                        "d p t -> p d t")),
                    reads=[(RD["d_q"], b * 2 + i_ // HLF,
                            b * 2 + i_ // HLF + 1)],
                    writes=[QIV], dma=True, tag=f"ldq{i_}")
            for c, (soff, swd) in enumerate(_chunks(LB, 512)):
                kj = c % 2
                KIV = (RW, kj * NKB * 512, (kj + 1) * NKB * 512)
                A("POOL" if c % 2 == 0 else "SP",
                  lambda e, jj=kj, so=soff, sw=swd: e.dma_start(
                      out=k_b[jj][:, :, 0:sw],
                      in_=kT_sp[b, :, :, so:so + sw].rearrange(
                          "d p t -> p d t")),
                  reads=[(RD["d_k"], b * 2 + soff // SH,
                          b * 2 + (soff + swd - 1) // SH + 1)],
                  writes=[KIV], dma=True, tag=f"ldk{p2}_{c}")
                for i_, qp in ((iA, 0), (iB, 1)):
                    L = (i_ + 1) * 128
                    if soff >= L:
                        continue
                    sw = min(swd, L - soff)
                    bank = qp
                    QIV = (RF, QTO + qp * NKB * 128,
                           QTO + (qp + 1) * NKB * 128)

                    def sc_group(e, jj=kj, bk=bank, sw_=sw, qq=qp):
                        last = None
                        for dk_ in range(NKB):
                            last = e.matmul(
                                mm[bk][:, 0:sw_], qtile[qq][:, dk_, :],
                                k_b[jj][:, dk_, 0:sw_],
                                start=(dk_ == 0), stop=(dk_ == NKB - 1))
                        return last
                    A("PE", sc_group, reads=[QIV, KIV], writes=[MMR[bank]],
                      tag=f"sc{i_}_{c}")
                    sbuf_sc = scoresP[qp]
                    SCV = (RF, qp * 2 * S, (qp + 1) * 2 * S)
                    dstart = L - 128
                    if soff + sw <= dstart:
                        A("DVE", lambda e, bk=bank, so=soff, sw_=sw, sb=sbuf_sc:
                          e.tensor_scalar(sb[:, so:so + sw_],
                                          mm[bk][:, 0:sw_], rscale, None,
                                          ALU.mult),
                          reads=[MMR[bank]], writes=[SCV],
                          tag=f"scev{i_}_{c}")
                    else:
                        w0 = dstart - soff
                        if w0 > 0:
                            A("DVE", lambda e, bk=bank, so=soff, w_=w0,
                              sb=sbuf_sc:
                              e.tensor_scalar(sb[:, so:so + w_],
                                              mm[bk][:, 0:w_], rscale, None,
                                              ALU.mult),
                              reads=[MMR[bank]], writes=[SCV])
                        A("DVE", lambda e, bk=bank, so=soff, w_=w0,
                          sb=sbuf_sc:
                          e.scalar_tensor_tensor(
                              sb[:, so + w_:so + w_ + 128],
                              mm[bk][:, w_:w_ + 128], rscale, causal[:, :],
                              ALU.mult, ALU.add),
                          reads=[MMR[bank], sm("causal")], writes=[SCV],
                          tag=f"scdg{i_}")

        def b_softmax(i):
            L = (i + 1) * 128
            qp = i % 2
            sb = scoresP[qp]
            SCV = (RF, qp * 2 * S, (qp + 1) * 2 * S)
            A("DVE", lambda e, LL=L, sb_=sb: e.reduce_max(
                nmx[:, :], sb_[:, 0:LL], axis=AXX, negate=True),
                reads=[SCV], writes=[sm("nmx")])
            A("ACT", lambda e, LL=L, sb_=sb: e.activation(
                sb_[:, 0:LL], sb_[:, 0:LL], AF.Exp, bias=nmx[:, 0:1],
                accum_out=ssm[:, :]),
                reads=[SCV, sm("nmx")],
                writes=[SCV, sm("ssm")], tag=f"exp{i}")
            A("DVE", lambda e: e.reciprocal(rsb[:, :], ssm[:, :]),
              reads=[sm("ssm")], writes=[sm("rsb")])
            A("DVE", lambda e, LL=L, sb_=sb: e.tensor_scalar(
                probs_bf[:, 0:LL], sb_[:, 0:LL], rsb[:, 0:1], None,
                ALU.mult),
                reads=[SCV, sm("rsb")],
                writes=[(RF, 4 * S, 5 * S)], tag=f"pn{i}")

        def b_probsT(i):
            for tkb in range(i + 1):
                tj = tkb % 4
                A("PE", lambda e, tk=tkb, tj_=tj: e.transpose(
                    tpp[tj_][:, 0:128],
                    probs_bf[:, tk * 128:(tk + 1) * 128], idbf[:, :]),
                    reads=[(RF, 4 * S, 5 * S), sm("idbf")],
                    writes=[TPR[tj]], tag=f"ptp{i}_{tkb}")
                A("ACT", lambda e, tk=tkb, tj_=tj: e.copy(
                    probsT[:, tk, :], tpp[tj_][:, 0:128]),
                    reads=[TPR[tj]],
                    writes=[(RF, 5 * S + tkb * 128, 5 * S + tkb * 128 + 128)])

        def b_av(i):
            PT = (RF, 5 * S, 5 * S + (i + 1) * 128)
            apar = i % 2
            for dc in range(NKB):
                bank = 2 + dc % 2

                def av_group(e, dc_=dc, bk=bank, ii=i):
                    last = None
                    for tk in range(ii + 1):
                        last = e.matmul(
                            mm[bk][:, 0:128],
                            vres[:, tk, dc_ * 128:(dc_ + 1) * 128],
                            probsT[:, tk, :],
                            start=(tk == 0), stop=(tk == ii))
                    return last
                A("PE", av_group, reads=[V_ALL, PT], writes=[MMR[bank]],
                  tag=f"av{i}_{dc}")
                A("DVE", lambda e, dc_=dc, bk=bank, ap=apar: e.tensor_copy(
                    avstage[ap][:, dc_, :], mm[bk][:, 0:128]),
                    reads=[MMR[bank]],
                    writes=[(RS, apar * NKB * 128 + dc * 128,
                             apar * NKB * 128 + dc * 128 + 128)])
            A("SP", lambda e, ii=i, ap=apar: e.dma_start(
                out=avT_sp[b, :, :, ii * 128:(ii + 1) * 128].rearrange(
                    "d p t -> p d t"),
                in_=avstage[ap][:, :, :]),
                reads=[(RS, apar * NKB * 128, (apar + 1) * NKB * 128)],
                writes=[(RD["d_avT"], b * NT + i, b * NT + i + 1)],
                dma=True, tag=f"spav{i}")

        prev = None
        for p2 in range(NT // 2):
            iA, iB = 2 * p2, 2 * p2 + 1
            b_scores_pair(p2)
            b_softmax(iA)
            if prev is not None:
                b_av(prev)
            b_probsT(iA)
            b_softmax(iB)
            b_av(iA)
            b_probsT(iB)
            prev = iB
            if p2 == NT // 2 - 1:
                for hc_, (ho, hw) in enumerate(_chunks(H, 512)):
                    A("POOL", lambda e, ho_=ho, hw_=hw: e.dma_start(
                        out=wo_full[:, :, ho_:ho_ + hw_],
                        in_=Wo[:, ho_:ho_ + hw_].rearrange(
                            "(b p) c -> p b c", p=128)),
                        writes=[(RW, 0, NKB * H)], dma=True,
                        tag=f"ldwo{hc_}")
        b_av(prev)

        WOIV = (RW, 0, NKB * H)
        u_full = [r_u[:, j, 0:H] for j in range(2)]
        s_full = [r_x[:, j, 512:512 + (H + 1) // 2].bitcast(BF16)
                  for j in range(2)]
        for ti in range(NT):
            par = ti % 2
            AIV = (RS, par * NKB * 128, (par + 1) * NKB * 128)
            A("SP", lambda e, tt=ti, pp=par: e.dma_start(
                out=av_b[pp][:, :, :],
                in_=avT_sp[b, :, :, tt * 128:(tt + 1) * 128].rearrange(
                    "d p t -> p d t")),
                reads=[(RD["d_avT"], b * NT + ti, b * NT + ti + 1)],
                writes=[AIV], dma=True, tag=f"ldav{ti}")
            A("ACT", lambda e, tt=ti, pp=par: e.dma_start(
                out=u_full[pp][0:128, :],
                in_=u_sp[b, tt * 128:(tt + 1) * 128, :]),
                reads=[(RD["d_u"], b * NT + ti, b * NT + ti + 1)],
                writes=[ru_iv(par, 0, H)], dma=True)
            for hc, (ho, hw) in enumerate(_chunks(H, 512)):
                bank = (ti * len(_chunks(H, 512)) + hc) % 4

                def o_group(e, tt=ti, pp=par, bk=bank, ho_=ho, hw_=hw):
                    last = None
                    for dk_ in range(NKB):
                        last = e.matmul(
                            mm[bk][:, 0:hw_], av_b[pp][:, dk_, :],
                            wo_full[:, dk_, ho_:ho_ + hw_],
                            start=(dk_ == 0), stop=(dk_ == NKB - 1))
                    return last
                A("PE", o_group, reads=[AIV, WOIV], writes=[MMR[bank]],
                  tag=f"omm{ti}_{hc}")
                A("DVE", lambda e, bk=bank, pp=par, ho_=ho, hw_=hw:
                  e.tensor_tensor(
                      s_full[pp][0:128, ho_:ho_ + hw_], mm[bk][:, 0:hw_],
                      u_full[pp][0:128, ho_:ho_ + hw_], ALU.add),
                  reads=[MMR[bank], ru_iv(par, 0, H)],
                  writes=[rx_iv(par, 512, 512 + (H + 1) // 2)],
                  tag=f"sev{ti}_{hc}")
            A("ACT", lambda e, tt=ti, pp=par: e.dma_start(
                out=s_sp[b, tt * 128:(tt + 1) * 128, :],
                in_=s_full[pp][0:128, :]),
                reads=[rx_iv(par, 512, 512 + (H + 1) // 2)],
                writes=[(RD["d_s"], b * NT + ti, b * NT + ti + 1)],
                dma=True)

        for g in range(2):
            gt0 = g * HLF
            FEAT_G = (RF, 0, NHB * SH)
            A("ACT", lambda e, gg=g: e.dma_start(
                out=featA[:, :, 0:GW],
                in_=xnT_sp[b, :, :, gg * GW:(gg + 1) * GW].rearrange(
                    "h p t -> p h t")),
                reads=[(RD["d_xnT"], b * 2 + g, b * 2 + g + 1)],
                writes=[FEAT_G], dma=True, tag=f"ldxnT{g}")
            R8 = (RS, 0, 2 * HLF * E)
            P8 = (RS, 2 * HLF * E, 4 * HLF * E)
            C8 = (RS, 4 * HLF * E, 5 * HLF * E)
            nm8 = stc[:, 0, 8:8 + HLF]
            sm8 = stc[:, 1, 8:8 + HLF]

            def bc8(ap2d):
                return ap2d[:, :, None].broadcast_to([128, HLF, E])
            for ti in range(HLF):
                bank = ti % 4

                def r_group(e, ti_=ti, bk=bank):
                    last = None
                    for hh in range(NHB):
                        last = e.matmul(
                            mm[bk][:, 0:E],
                            featA[:, hh, ti_ * 128:(ti_ + 1) * 128],
                            wr_sb[:, hh, :], start=(hh == 0),
                            stop=(hh == NHB - 1))
                    return last
                A("PE", r_group, reads=[FEAT_G, sm("wr")],
                  writes=[MMR[bank]], tag=f"rtr{ti}")
                A("ACT", lambda e, ti_=ti, bk=bank: e.copy(
                    rtr8[:, ti_, :], mm[bk][:, 0:E]),
                    reads=[MMR[bank]], writes=[R8], tag=f"rtrev{ti}")
            A("DVE", lambda e: e.reduce_max(nm8, rtr8[:, :, :], axis=AXX),
              reads=[R8], writes=[sm("stc", 8, 16)])
            A("DVE", lambda e: e.tensor_tensor(
                rtr8[:, :, :], rtr8[:, :, :], bc8(nm8), ALU.subtract),
                reads=[R8, sm("stc", 8, 16)], writes=[R8])
            A("ACT", lambda e: e.activation(rtr8[:, :, :], rtr8[:, :, :],
                                            AF.Exp),
              reads=[R8], writes=[R8], tag="rexp")
            A("DVE", lambda e: e.reduce_sum(sm8, rtr8[:, :, :], axis=AXX),
              reads=[R8], writes=[sm("stc", 24, 32)])
            A("DVE", lambda e: e.reciprocal(sm8, sm8),
              reads=[sm("stc", 24, 32)], writes=[sm("stc", 24, 32)])
            A("DVE", lambda e: e.tensor_tensor(
                rtr8[:, :, :], rtr8[:, :, :], bc8(sm8), ALU.mult),
                reads=[R8, sm("stc", 24, 32)], writes=[R8])
            A("DVE", lambda e: e.tensor_copy(prw8[:, :, :], rtr8[:, :, :]),
              reads=[R8], writes=[P8])
            for j in range(4):
                A("DVE", lambda e: e.reduce_max(nm8, prw8[:, :, :],
                                                axis=AXX),
                  reads=[P8], writes=[sm("stc", 8, 16)])
                if j < 3:
                    A("DVE", lambda e: e.tensor_tensor(
                        cw8[:, :, :], prw8[:, :, :], bc8(nm8), ALU.is_lt),
                        reads=[P8, sm("stc", 8, 16)], writes=[C8])
                    A("DVE", lambda e: e.tensor_tensor(
                        prw8[:, :, :], prw8[:, :, :], cw8[:, :, :],
                        ALU.mult),
                        reads=[P8, C8], writes=[P8])
            A("DVE", lambda e: e.tensor_tensor(
                prw8[:, :, :], rtr8[:, :, :], bc8(nm8), ALU.is_ge),
                reads=[R8, sm("stc", 8, 16)], writes=[P8])
            A("DVE", lambda e: e.tensor_tensor(
                prw8[:, :, :], prw8[:, :, :], rtr8[:, :, :], ALU.mult),
                reads=[P8, R8], writes=[P8])
            A("DVE", lambda e: e.reduce_sum(sm8, prw8[:, :, :], axis=AXX),
              reads=[P8], writes=[sm("stc", 24, 32)])
            A("DVE", lambda e: e.tensor_scalar_add(sm8, sm8, 1e-8),
              reads=[sm("stc", 24, 32)], writes=[sm("stc", 24, 32)])
            A("DVE", lambda e: e.reciprocal(sm8, sm8),
              reads=[sm("stc", 24, 32)], writes=[sm("stc", 24, 32)])
            A("DVE", lambda e: e.tensor_tensor(
                prw8[:, :, :], prw8[:, :, :], bc8(sm8), ALU.mult),
                reads=[P8, sm("stc", 24, 32)], writes=[P8])
            A("DVE", lambda e: e.tensor_tensor(
                cw8[:, :, :], prw8[:, :, :],
                es_b[:, None, :].broadcast_to([128, HLF, E]), ALU.mult),
                reads=[P8, sm("esb")], writes=[C8], tag="cw8")
            def d1_tail(ti):
                tj = ti % 2
                tk2 = 2 + ti % 2
                A("PE", lambda e, ti_=ti, t_=tj: e.transpose(
                    tpp[t_][:, 0:E], cw8[:, ti_, :], idbf[:, :]),
                    reads=[C8, sm("idbf")], writes=[TPR[tj]])
                A("ACT", lambda e, t_=tj, ti_=ti: e.copy(
                    cwT[:, ti_ % 2, :], tpp[t_][:, 0:E]),
                  reads=[TPR[tj]], writes=[sm("cwT", ti % 2, ti % 2 + 1)])
                wb = (ti % 2) * 2
                A("PE", lambda e, bk=wb, ti_=ti: e.matmul(
                    mm[bk][:, 0:K], cwT[:, ti_ % 2, :], mix_bf[:, :],
                    start=True, stop=True),
                    reads=[sm("cwT", ti % 2, ti % 2 + 1), sm("mixbf")],
                    writes=[MMR[wb]], tag=f"wkmm{ti}")
                A("ACT", lambda e, ti_=ti, bk=wb: e.copy(
                    wk_g[:, ti_, :], mm[bk][:, 0:K]),
                    reads=[MMR[wb]], writes=[sm("wkg", ti, ti + 1)],
                    tag=f"wkev{ti}")
            for ch, (co, cwd) in enumerate(_chunks(KI, min(512, I))):
                wj = ch % 2
                kk = co // I
                WGIV = (RW, wj * NHB * 512, (wj + 1) * NHB * 512)
                WUIV = (RW, (2 + wj) * NHB * 512, (3 + wj) * NHB * 512)
                A("POOL", lambda e, jj=wj, co_=co, cw_=cwd: e.dma_start(
                    out=wg_st[jj][:, :, 0:cw_],
                    in_=Wg[:, co_:co_ + cw_].rearrange(
                        "(b p) c -> p b c", p=128)),
                    writes=[WGIV], dma=True, tag=f"ldwg{ch}")
                A("POOL", lambda e, jj=wj, co_=co, cw_=cwd: e.dma_start(
                    out=wu_st[jj][:, :, 0:cw_],
                    in_=Wu[:, co_:co_ + cw_].rearrange(
                        "(b p) c -> p b c", p=128)),
                    writes=[WUIV], dma=True, tag=f"ldwu{ch}")
                for ti in range(HLF):
                    par = ti % 2
                    bg = par * 2
                    bu = par * 2 + 1

                    def g_group(e, jj=wj, bk=bg, ti_=ti, cw_=cwd):
                        last = None
                        for hh in range(NHB):
                            last = e.matmul(
                                mm[bk][:, 0:cw_],
                                featA[:, hh, ti_ * 128:(ti_ + 1) * 128],
                                wg_st[jj][:, hh, 0:cw_],
                                start=(hh == 0), stop=(hh == NHB - 1))
                        return last
                    A("PE", g_group, reads=[WGIV, FEAT_G], writes=[MMR[bg]],
                      tag=f"gmm{ch}_{ti}")

                    def u_group(e, jj=wj, bk=bu, ti_=ti, cw_=cwd):
                        last = None
                        for hh in range(NHB):
                            last = e.matmul(
                                mm[bk][:, 0:cw_],
                                featA[:, hh, ti_ * 128:(ti_ + 1) * 128],
                                wu_st[jj][:, hh, 0:cw_],
                                start=(hh == 0), stop=(hh == NHB - 1))
                        return last
                    A("PE", u_group, reads=[WUIV, FEAT_G], writes=[MMR[bu]],
                      tag=f"umm{ch}_{ti}")
                    if ch == 0:
                        d1_tail(ti)
                    A("ACT", lambda e, bk=bg, pp=par, cw_=cwd: e.activation(
                        sgsD[pp][:, 0:cw_], mm[bk][:, 0:cw_], AF.Silu),
                        reads=[MMR[bg]], writes=[ru_iv(par, 0, 1024)],
                        tag=f"silu{ch}_{ti}")
                    A("DVE", lambda e, bk=bu, pp=par, ti_=ti, kk_=kk,
                      cw_=cwd: e.scalar_tensor_tensor(
                          tbD[pp][:, 0:cw_], sgsD[pp][:, 0:cw_],
                          wk_g[:, ti_, kk_:kk_ + 1], mm[bk][:, 0:cw_],
                          ALU.mult, ALU.mult),
                      reads=[ru_iv(par, 0, 1024), sm("wkg", ti, ti + 1),
                             MMR[bu]],
                      writes=[ru_iv(par, 1024, 2048)], tag=f"tb{ch}_{ti}")
                    ci0 = co % I
                    CIV = (RV, 2 * (ti * I + ci0), 2 * (ti * I + ci0 + cwd))
                    if kk == 0:
                        A("DVE", lambda e, pp=par, ti_=ti, ci=ci0, cw_=cwd:
                          e.tensor_copy(cacc[:, ti_, ci:ci + cw_],
                                        tbD[pp][:, 0:cw_]),
                          reads=[ru_iv(par, 1024, 2048)], writes=[CIV],
                          tag=f"cset{ch}_{ti}")
                    else:
                        A("DVE", lambda e, pp=par, ti_=ti, ci=ci0, cw_=cwd:
                          e.tensor_tensor(cacc[:, ti_, ci:ci + cw_],
                                          cacc[:, ti_, ci:ci + cw_],
                                          tbD[pp][:, 0:cw_], ALU.add),
                          reads=[ru_iv(par, 1024, 2048), CIV], writes=[CIV],
                          tag=f"cacc{ch}_{ti}")
            for ti in range(HLF):
                CAV = (RV, 2 * ti * I, 2 * (ti + 1) * I)
                A("DVE", lambda e, ti_=ti: e.tensor_copy(
                    cbf[:, :], cacc[:, ti_, :]),
                    reads=[CAV], writes=[(RV, CBO, CBO + I)], tag=f"cbf{ti}")
                for ib in range(NIB):
                    tj = ib % 4
                    A("PE", lambda e, ib_=ib, tj_=tj: e.transpose(
                        tpp[tj_][:, 0:128],
                        cbf[:, ib_ * 128:(ib_ + 1) * 128], idbf[:, :]),
                        reads=[(RV, CBO, CBO + I), sm("idbf")],
                        writes=[TPR[tj]], tag=f"ctp{ti}_{ib}")
                    A("DVE", lambda e, ib_=ib, tj_=tj, ti_=ti: e.tensor_copy(
                        cT[:, ib_, ti_ * 128:(ti_ + 1) * 128],
                        tpp[tj_][:, 0:128]),
                        reads=[TPR[tj]],
                        writes=[(RV, CTO + ib * GW + ti * 128,
                                 CTO + ib * GW + ti * 128 + 128)])
            CT_ALL = (RV, CTO, CTO + NIB * GW)
            for ch in range(NI2B):
                wj = ch % 2
                WSGIV = (RW, WSGO + wj * NHB * 128,
                         WSGO + (wj + 1) * NHB * 128)
                WSUIV = (RW, WSUO + wj * NHB * 128,
                         WSUO + (wj + 1) * NHB * 128)
                A("POOL", lambda e, jj=wj, ch_=ch: e.dma_start(
                    out=wsg_st[jj][:, :, :],
                    in_=Wsg[:, ch_ * 128:(ch_ + 1) * 128].rearrange(
                        "(b p) c -> p b c", p=128)),
                    writes=[WSGIV], dma=True, tag=f"ldwsg{ch}")
                A("POOL", lambda e, jj=wj, ch_=ch: e.dma_start(
                    out=wsu_st[jj][:, :, :],
                    in_=Wsu[:, ch_ * 128:(ch_ + 1) * 128].rearrange(
                        "(b p) c -> p b c", p=128)),
                    writes=[WSUIV], dma=True, tag=f"ldwsu{ch}")
                for ts_, (toff, tw) in enumerate(_chunks(GW, 512)):
                    par = ts_ % 2
                    bg = par * 2
                    bu = par * 2 + 1

                    def sg_group(e, jj=wj, bk=bg, to=toff, tw_=tw):
                        last = None
                        for hh in range(NHB):
                            last = e.matmul(
                                mm[bk][:, 0:tw_], wsg_st[jj][:, hh, :],
                                featA[:, hh, to:to + tw_],
                                start=(hh == 0), stop=(hh == NHB - 1))
                        return last
                    A("PE", sg_group, reads=[WSGIV, FEAT_G],
                      writes=[MMR[bg]], tag=f"sgmm{ch}_{ts_}")

                    def su_group(e, jj=wj, bk=bu, to=toff, tw_=tw):
                        last = None
                        for hh in range(NHB):
                            last = e.matmul(
                                mm[bk][:, 0:tw_], wsu_st[jj][:, hh, :],
                                featA[:, hh, to:to + tw_],
                                start=(hh == 0), stop=(hh == NHB - 1))
                        return last
                    A("PE", su_group, reads=[WSUIV, FEAT_G],
                      writes=[MMR[bu]], tag=f"summ{ch}_{ts_}")
                    A("ACT", lambda e, bk=bg, pp=par, tw_=tw: e.activation(
                        sgs2[pp][:, 0:tw_], mm[bk][:, 0:tw_], AF.Silu),
                        reads=[MMR[bg]],
                        writes=[(RV, SG2O + par * 1024,
                                 SG2O + (par + 1) * 1024)],
                        tag=f"ssil{ch}_{ts_}")
                    A("DVE", lambda e, bk=bu, pp=par, ch_=ch, to=toff,
                      tw_=tw: e.tensor_tensor(
                          hshT[:, ch_, to:to + tw_], sgs2[pp][:, 0:tw_],
                          mm[bk][:, 0:tw_], ALU.mult),
                      reads=[(RV, SG2O + par * 1024,
                              SG2O + (par + 1) * 1024), MMR[bu]],
                      writes=[(RW, ch * GW + toff, ch * GW + toff + tw)],
                      tag=f"hsh{ch}_{ts_}")
            HSH_ALL = (RW, 0, NI2B * GW)
            for hc, (ho, hw) in enumerate(_chunks(H, 512)):
                wj = hc % 2
                WDIV = (RV, wj * NIB * 512, (wj + 1) * NIB * 512)
                WSDIV = (RV, WSDO + wj * NI2B * 512,
                         WSDO + (wj + 1) * NI2B * 512)
                A("POOL", lambda e, jj=wj, ho_=ho, hw_=hw: e.dma_start(
                    out=wd_st[jj][:, :, 0:hw_],
                    in_=Wd[:, ho_:ho_ + hw_].rearrange(
                        "(b p) c -> p b c", p=128)),
                    writes=[WDIV], dma=True, tag=f"ldwd{hc}")
                A("POOL", lambda e, jj=wj, ho_=ho, hw_=hw: e.dma_start(
                    out=wsd_st[jj][:, :, 0:hw_],
                    in_=Wsd[:, ho_:ho_ + hw_].rearrange(
                        "(b p) c -> p b c", p=128)),
                    writes=[WSDIV], dma=True, tag=f"ldwsd{hc}")
                for ti in range(HLF):
                    t = gt0 + ti
                    par = ti % 2
                    bank = (hc * HLF + ti) % 4
                    A("ACT", lambda e, tt=t, pp=par, ho_=ho, hw_=hw:
                      e.dma_start(
                          out=x_sl[pp][0:128, 0:hw_],
                          in_=X[b, tt * 128:(tt + 1) * 128, ho_:ho_ + hw_]),
                      writes=[rx_iv(par, 0, 512)], dma=True)
                    A("ACT", lambda e, tt=t, pp=par, ho_=ho, hw_=hw:
                      e.dma_start(
                          out=s_sl[pp][0:128, 0:hw_],
                          in_=s_sp[b, tt * 128:(tt + 1) * 128,
                                   ho_:ho_ + hw_]),
                      reads=[(RD["d_s"], b * NT + t, b * NT + t + 1)],
                      writes=[ru_iv(par, 512, 1024)], dma=True)

                    def d_group(e, jj=wj, bk=bank, ti_=ti, hw_=hw):
                        last = None
                        for ib_ in range(NIB):
                            last = e.matmul(
                                mm[bk][:, 0:hw_],
                                cT[:, ib_, ti_ * 128:(ti_ + 1) * 128],
                                wd_st[jj][:, ib_, 0:hw_],
                                start=(ib_ == 0), stop=False)
                        for i2_ in range(NI2B):
                            last = e.matmul(
                                mm[bk][:, 0:hw_],
                                hshT[:, i2_, ti_ * 128:(ti_ + 1) * 128],
                                wsd_st[jj][:, i2_, 0:hw_],
                                start=False, stop=(i2_ == NI2B - 1))
                        return last
                    A("PE", d_group,
                      reads=[CT_ALL, WDIV, HSH_ALL, WSDIV],
                      writes=[MMR[bank]], tag=f"dmm{hc}_{ti}")
                    A("ACT", lambda e, bk=bank, pp=par, hw_=hw: e.activation(
                        gsb[pp][:, 0:hw_], mm[bk][:, 0:hw_], AF.Silu),
                        reads=[MMR[bank]], writes=[rx_iv(par, 512, 1024)],
                        tag=f"gsil{hc}_{ti}")
                    A("DVE", lambda e, pp=par, hw_=hw: e.tensor_tensor(
                        tb2[pp][:, 0:hw_], gsb[pp][:, 0:hw_],
                        s_sl[pp][0:128, 0:hw_], ALU.mult),
                        reads=[rx_iv(par, 512, 1024), ru_iv(par, 512, 1024)],
                        writes=[rx_iv(par, 1024, 1536)], tag=f"sg{hc}_{ti}")
                    A("DVE", lambda e, pp=par, hw_=hw: e.tensor_tensor(
                        ostage[pp][:, 0:hw_], tb2[pp][:, 0:hw_],
                        x_sl[pp][0:128, 0:hw_], ALU.add),
                        reads=[rx_iv(par, 1024, 1536), rx_iv(par, 0, 512)],
                        writes=[rx_iv(par, 1536, 2048)], tag=f"fin{hc}_{ti}")
                    A("ACT", lambda e, tt=t, pp=par, ho_=ho, hw_=hw:
                      e.dma_start(
                          out=OUT[b, tt * 128:(tt + 1) * 128, ho_:ho_ + hw_],
                          in_=ostage[pp][:, 0:hw_]),
                      reads=[rx_iv(par, 1536, 2048)],
                      writes=[dr("d_out", b)], dma=True,
                      tag=f"stout{hc}_{ti}")

    for _b in range(B_CORE):
        emit_batch(_b)

    with nc.Block() as block:
        s.finalize(block, st)
    return st


def kernel(**inputs):
    D = REAL_DIMS
    n_cores = 8
    x = np.ascontiguousarray(inputs["L0_attn_x"], dtype=np.float32)
    xs = np.split(x, n_cores, axis=0)
    nc = bass.Bass("TRN2", target_bir_lowering=False)
    build_program(nc, D)
    weights = {k: np.ascontiguousarray(v, dtype=np.float32)
               for k, v in inputs.items()
               if k not in ("L0_attn_x", "L0_attn_gamma", "L0_attn_beta")}
    weights["gamma_pb"] = np.ascontiguousarray(
        np.asarray(inputs["L0_attn_gamma"], dtype=np.float32)
        .reshape(-1, 128).T)
    weights["beta_pb"] = np.ascontiguousarray(
        np.asarray(inputs["L0_attn_beta"], dtype=np.float32)
        .reshape(-1, 128).T)
    in_maps = [dict(weights, L0_attn_x=xs[c]) for c in range(n_cores)]
    res = run_bass_kernel_spmd(nc, in_maps, core_ids=list(range(n_cores)))
    return np.concatenate([r["out"] for r in res.results], axis=0)
